# version 21
# baseline (speedup 1.0000x reference)
import numpy as np
import ml_dtypes
from contextlib import ExitStack
import concourse.bass as bass
import concourse.mybir as mybir
from concourse.bass_utils import run_bass_kernel_spmd
from concourse.bass_core_rust import IndirectOffsetOnAxis

F32 = mybir.dt.float32
BF16 = mybir.dt.bfloat16
I32 = mybir.dt.int32
U32 = mybir.dt.uint32
AF = mybir.ActivationFunctionType
ALU = mybir.AluOpType
AX = mybir.AxisListType

ENGINES = ("pe", "act", "dve", "pool", "sp")


class Tok:
    __slots__ = ("sem", "val")

    def __init__(self, sem, val):
        self.sem = sem
        self.val = val


class DmaGroup:
    __slots__ = ("sem", "val")

    def __init__(self, sem):
        self.sem = sem
        self.val = 0

    def snap(self):
        return Tok(self.sem, self.val)


class Prog:
    def __init__(self, nc, ctx):
        self.nc = nc
        self.ctx = ctx
        self.sems = {e: ctx.enter_context(nc.semaphore("seq_" + e))
                     for e in ENGINES}
        self.counts = {e: 0 for e in ENGINES}
        self.thunks = {e: [] for e in ENGINES}
        self.watermark = {e: {} for e in ENGINES}
        self._group_n = 0

    def new_sem(self, name):
        return self.ctx.enter_context(self.nc.semaphore(name))

    def dma_group(self, name=None):
        self._group_n += 1
        sem = self.new_sem(name or f"dmag_{self._group_n}")
        return DmaGroup(sem)

    def _waits_for(self, eng, deps):
        wm = self.watermark[eng]
        needed = []
        for d in deps:
            if d is None:
                continue
            if isinstance(d, DmaGroup):
                d = d.snap()
            sem, val = d.sem, d.val
            key = id(sem)
            if val > wm.get(key, 0):
                wm[key] = val
                needed.append((sem, val))
        return needed

    def emit(self, eng, fn, *args, deps=(), track=True, **kwargs):
        my_sem = self.sems[eng]
        if track:
            self.counts[eng] += 1
        needed = self._waits_for(eng, deps)
        my_inc = 1 if track else 0

        def thunk(engine, needed=needed, fn=fn, args=args, kwargs=kwargs,
                  my_inc=my_inc, my_sem=my_sem):
            for (s, v) in needed:
                engine.wait_ge(s, v)
            inst = getattr(engine, fn)(*args, **kwargs)
            if my_inc:
                inst.then_inc(my_sem, 1)
            return inst

        self.thunks[eng].append(thunk)
        return Tok(my_sem, self.counts[eng]) if track else None

    def dma(self, eng, *args, deps=(), group=None, **kwargs):
        if group is None:
            group = self.dma_group()
        needed = self._waits_for(eng, deps)
        group.val += 16
        gsem = group.sem

        def thunk(engine, needed=needed, args=args, kwargs=kwargs, gsem=gsem):
            for (s, v) in needed:
                engine.wait_ge(s, v)
            engine.dma_start(*args, **kwargs).then_inc(gsem, 16)

        self.thunks[eng].append(thunk)
        return group

    def indirect_dma(self, *args, deps=(), group=None, **kwargs):
        if group is None:
            group = self.dma_group()
        needed = self._waits_for("pool", deps)
        group.val += 16
        gsem = group.sem

        def thunk(engine, needed=needed, args=args, kwargs=kwargs, gsem=gsem):
            for (s, v) in needed:
                engine.wait_ge(s, v)
            engine.indirect_dma_start(*args, **kwargs).then_inc(gsem, 16)

        self.thunks["pool"].append(thunk)
        return group

    def indirect_dma_breg(self, bc_state, *args, deps=(), group=None, **kwargs):
        if group is None:
            group = self.dma_group()
        needed = self._waits_for("pool", deps)
        group.val += 16
        gsem = group.sem

        def thunk(engine, needed=needed, args=args, kwargs=kwargs, gsem=gsem):
            for (s, v) in needed:
                engine.wait_ge(s, v)
            engine.indirect_dma_start(
                *args, bounds_check=bc_state["reg"], **kwargs
            ).then_inc(gsem, 16)

        self.thunks["pool"].append(thunk)
        return group

    def collective(self, kind, op, groups, in_ap, out_ap, deps=()):
        self._group_n += 1
        sem = self.new_sem(f"cc_{self._group_n}")
        needed = self._waits_for("pool", deps)

        def thunk(engine, needed=needed):
            for (s, v) in needed:
                engine.wait_ge(s, v)
            engine.collective_compute(
                kind, op, replica_groups=groups,
                ins=[in_ap.opt()], outs=[out_ap.opt()],
            ).then_inc(sem)

        self.thunks["pool"].append(thunk)
        return Tok(sem, 1)

    def bar(self, eng):
        return Tok(self.sems[eng], self.counts[eng])

    def wait_raw(self, eng, tok):
        if isinstance(tok, DmaGroup):
            tok = tok.snap()

        def thunk(engine, tok=tok):
            engine.wait_ge(tok.sem, tok.val)
        self.thunks[eng].append(thunk)

    def build(self, block):
        def run(eng_name):
            def body(engine):
                for t in self.thunks[eng_name]:
                    t(engine)
            return body
        if self.thunks["sp"]:
            block.sync(run("sp"))
        if self.thunks["pool"]:
            block.gpsimd(run("pool"))
        if self.thunks["dve"]:
            block.vector(run("dve"))
        if self.thunks["pe"]:
            block.tensor(run("pe"))
        if self.thunks["act"]:
            block.scalar(run("act"))


class Cfg:
    def __init__(self, S=8192, TPC=2048, CAP=256, INTER=768):
        self.D = 1024
        self.E = 64
        self.EPC = 8
        self.S = S
        self.TPC = TPC
        self.CAP = CAP
        self.INTER = INTER
        self.MT = self.D // 128
        self.NT = TPC // 128
        self.KT = S // 128
        self.QCH = min(512, TPC)
        self.NQC = TPC // self.QCH
        self.GCH = min(512, TPC)
        self.NGC = S // self.GCH
        self.IT = INTER // 128
        self.SLOTS = self.E * CAP
        self.SPC = self.EPC * CAP
        self.LNEPS = 1e-5
        assert TPC % 128 == 0 and S % self.GCH == 0 and self.SPC % 128 == 0
        assert self.SPC == TPC, "layout reuse assumes SPC == TPC"


def build_program(c):
    nc = bass.Bass("TRN2", target_bir_lowering=False)
    D, E, EPC, CAP, INTER = c.D, c.E, c.EPC, c.CAP, c.INTER
    MT, NT, KT, IT = c.MT, c.NT, c.KT, c.IT
    TPC, S, QCH, NQC, GCH, NGC = c.TPC, c.S, c.QCH, c.NQC, c.GCH, c.NGC

    x_own = nc.dram_tensor("x_own", [TPC, D], F32, kind="ExternalInput").ap()
    w1 = nc.dram_tensor("w1", [D, 4 * D], F32, kind="ExternalInput").ap()
    b1 = nc.dram_tensor("b1", [4 * D], F32, kind="ExternalInput").ap()
    w2 = nc.dram_tensor("w2", [D, D], F32, kind="ExternalInput").ap()
    b2 = nc.dram_tensor("b2", [D], F32, kind="ExternalInput").ap()
    gamma = nc.dram_tensor("gamma", [D], F32, kind="ExternalInput").ap()
    beta = nc.dram_tensor("beta", [D], F32, kind="ExternalInput").ap()
    gate_w = nc.dram_tensor("gate_w", [E, D], F32, kind="ExternalInput").ap()
    wg8 = nc.dram_tensor("wg8", [EPC, INTER, D], F32, kind="ExternalInput").ap()
    wu8 = nc.dram_tensor("wu8", [EPC, INTER, D], F32, kind="ExternalInput").ap()
    wd8 = nc.dram_tensor("wd8", [EPC, D, INTER], F32, kind="ExternalInput").ap()
    rankmask = nc.dram_tensor("rankmask", [1, 8], F32, kind="ExternalInput").ap()
    out = nc.dram_tensor("out", [TPC, D], F32, kind="ExternalOutput").ap()

    xnT_mine_d = nc.dram_tensor("xnT_mine_d", [D, TPC], BF16, kind="Internal").ap()
    xnT_all_d = nc.dram_tensor("xnT_all_d", [4 * D, TPC], BF16, kind="Internal").ap()
    qT_d = nc.dram_tensor("qT_d", [D, TPC], BF16, kind="Internal").ap()
    uT_d = nc.dram_tensor("uT_d", [D, TPC], BF16, kind="Internal").ap()
    T_d = nc.dram_tensor("T_d", [8, E], F32, kind="Internal").ap()
    C_d = nc.dram_tensor("C_d", [1, E], F32, kind="Internal").ap()
    R_d = nc.dram_tensor("R_d", [NT, E], F32, kind="Internal").ap()
    disp_part = nc.dram_tensor("disp_part", [c.SLOTS, D], BF16, kind="Internal").ap()
    disp_own = nc.dram_tensor("disp_own", [c.SPC, D], BF16, kind="Internal").ap()
    wgbf_d = nc.dram_tensor("wgbf_d", [EPC, INTER, D], BF16, kind="Internal").ap()
    wubf_d = nc.dram_tensor("wubf_d", [EPC, INTER, D], BF16, kind="Internal").ap()
    wdbf_d = nc.dram_tensor("wdbf_d", [EPC, D, INTER], BF16, kind="Internal").ap()
    expc_flag_d = nc.dram_tensor("expc_flag_d", [EPC, 1], BF16,
                                 kind="Internal").ap()
    y_own_d = nc.dram_tensor("y_own_d", [c.SPC, D], BF16, kind="Internal").ap()
    avT_d = nc.dram_tensor("avT_d", [D, TPC], BF16, kind="Internal").ap()
    y_all_d = nc.dram_tensor("y_all_d", [c.SLOTS, D], BF16, kind="Internal").ap()

    with ExitStack() as ctx:
        ent = ctx.enter_context

        def sbuf(name, shape, dt=BF16):
            return ent(nc.sbuf_tensor(name, shape, dt))

        ARENA = 2 * IT * 128 * 8 + IT * D
        big0 = sbuf("big0", [128, NT * D])
        big1 = sbuf("big1", [128, MT * TPC])
        arena_a = sbuf("arena_a", [128, max(2 * IT * 128 * 8, 2 * S)])
        arena_b = sbuf("arena_b", [128, max(16384, KT * 256)])
        small4 = sbuf("small4", [128, 5120])
        wslot = sbuf("wslot", [128, MT * D])
        qtg = sbuf("qtg", [128, 2 * TPC])
        ujt = sbuf("ujt", [128, 2 * MT * 128])
        y_g0 = sbuf("y_g0", [128, D])
        y_g1 = sbuf("y_g1", [128, D])
        zt = sbuf("zt", [128, 2 * 128])
        z2t = sbuf("z2t", [128, 2 * 128])
        logits = sbuf("logits", [128, max(NT * E, D)], F32)
        A_sb = sbuf("A_sb", [128, NT * E])
        idxf = sbuf("idxf", [128, NT * 8], F32)
        w8 = sbuf("w8", [128, NT * 8], F32)
        posal = sbuf("posal", [128, NT * 8], F32)
        validm = sbuf("validm", [128, NT * 8], F32)
        idx32 = sbuf("idx32", [128, NT * 8], I32)
        scr64 = sbuf("scr64", [128, E], F32)
        eq_bf = sbuf("eq_bf", [128, E])
        t8 = sbuf("t8", [128, 8], F32)
        idxu = sbuf("idxu", [128, 8], U32)
        sml = sbuf("sml", [128, 32], F32)
        statc = sbuf("statc", [128, 16], F32)
        xg = sbuf("xg", [128, D], F32)
        iota_p = sbuf("iota_p", [128, 128], F32)
        iota_f = sbuf("iota_f", [128, 128], F32)
        iota64r = sbuf("iota64r", [128, E], F32)
        identb = sbuf("identb", [128, 128])
        identf = sbuf("identf", [64, 64], F32)
        triub = sbuf("triub", [128, 128])
        triu16 = sbuf("triu16", [16, 16], F32)
        ones_row = sbuf("ones_row", [1, 128], F32)
        ones_cb = sbuf("ones_cb", [128, 1])
        zrow = sbuf("zrow", [1, 512])
        avst = sbuf("avst", [128, 4 * 512])
        row_stage = sbuf("row_stage", [1, D], F32)
        rmask = sbuf("rmask", [1, 8], F32)
        bcols = sbuf("bcols", [128, 48], F32)
        gate_b = sbuf("gate_b", [E, D])
        gwT = sbuf("gwT", [128, MT * E])
        slT = sbuf("slT", [64, NT], F32)
        cs_sb = sbuf("cs_sb", [1, E], F32)
        T_sb = sbuf("T_sb", [8, E], F32)
        sl_sb = sbuf("sl_sb", [NT, E], F32)
        R_sb = sbuf("R_sb", [NT, E], F32)
        C_sb = sbuf("C_sb", [1, E], F32)
        c_rep = sbuf("c_rep", [128, D])
        b2p_rep = sbuf("b2p_rep", [128, D])
        Rrow = logits[0:1, 0:NT * E]
        crow = logits[0:1, 0:D]
        b2prow = xg[0:1, 0:D]

        psP2 = ent(nc.psum_tensor("psP2", [128, 1024], F32))
        psQ2 = ent(nc.psum_tensor("psQ2", [128, 1024], F32))
        psR2 = ent(nc.psum_tensor("psR2", [128, 1024], F32))
        psA = ent(nc.psum_tensor("psA", [128, 512], F32))
        psB = ent(nc.psum_tensor("psB", [128, 512], F32))
        psTrA = psA[:, 0:64].bitcast(BF16)
        psTrB = psB[:, 0:64].bitcast(BF16)

        p = Prog(nc, ctx)
        block = ent(nc.Block())
        grp_all = [list(range(8))]
        grp_batch = [[0, 1, 2, 3], [4, 5, 6, 7]]

        ti1 = p.emit("pool", "iota", out=iota_p[:], pattern=[[0, 128]], base=0,
                     channel_multiplier=1, allow_small_or_imprecise_dtypes=True)
        ti2 = p.emit("pool", "iota", out=iota_f[:], pattern=[[1, 128]], base=0,
                     channel_multiplier=0, allow_small_or_imprecise_dtypes=True)
        ti3 = p.emit("pool", "iota", out=iota64r[:], pattern=[[1, E]], base=0,
                     channel_multiplier=0, allow_small_or_imprecise_dtypes=True)
        tid = p.emit("dve", "tensor_tensor", out=identb[:], in0=iota_p[:],
                     in1=iota_f[:], op=ALU.is_equal, deps=[ti1, ti2])
        tidf = p.emit("dve", "tensor_tensor", out=identf[:],
                      in0=iota_p[0:64, 0:64], in1=iota_f[0:64, 0:64],
                      op=ALU.is_equal, deps=[ti1, ti2])
        ttri = p.emit("dve", "tensor_tensor", out=triub[:], in0=iota_p[:],
                      in1=iota_f[:], op=ALU.is_lt, deps=[ti1, ti2])
        ttri16 = p.emit("dve", "tensor_tensor", out=triu16[:],
                        in0=iota_p[0:16, 0:16], in1=iota_f[0:16, 0:16],
                        op=ALU.is_lt, deps=[ti1, ti2])
        tones = p.emit("dve", "memset", ones_row[:], 1.0)
        tonesc = p.emit("dve", "memset", ones_cb[:], 1.0)
        tyg0 = p.emit("dve", "memset", y_g0[:], 0.0)
        tyg1 = p.emit("dve", "memset", y_g1[:], 0.0)
        tzrow = p.emit("dve", "memset", zrow[:], 0.0)
        zrow_bf = zrow
        ones_row_bf = sbuf("ones_row_bf", [1, 128])
        tones_bf = p.emit("dve", "memset", ones_row_bf[:], 1.0)
        teps = p.emit("dve", "memset", sml[:, 15:16], float(c.LNEPS))

        g_rows = p.dma_group("g_rows")
        p.dma("sp", out=rmask[:], in_=rankmask, group=g_rows)
        g_gate = p.dma("pool", out=gate_b[:], in_=gate_w)

        g_expc = p.dma_group("g_expc")
        g_expc_e = [p.dma_group(f"g_expc{e}") for e in range(EPC)]
        tz0 = p.emit("dve", "memset", arena_b[:, 0:16384], 0.0)
        g_zero = p.dma_group("g_zero")

        row_srcs = [(b1, 3 * D, 8), (b1, 1 * D, 8), (b1, 2 * D, 8),
                    (b1, 0, 8), (gamma, 0, 8), (beta, 0, 8)]
        g_rst = p.dma_group("g_rst")
        tcolmm = None
        coli = 0
        prev_mm = None
        for (src, off, njj) in row_srcs:
            grl = p.dma("sp", out=row_stage[:],
                        in_=src.rearrange("(o n) -> o n", o=1)[:, off:off + D],
                        deps=[prev_mm], group=g_rst)
            for jj in range(njj):
                tcolmm = p.emit("pe", "matmul", psA[:, coli:coli + 1],
                                row_stage[0:1, 128 * jj:128 * (jj + 1)],
                                ones_row[0:1, 0:1],
                                start=(coli == 0), stop=(coli == 47),
                                deps=[g_rst, tones])
                coli += 1
            prev_mm = tcolmm
        tbcols = p.emit("dve", "tensor_copy", out=bcols[:], in_=psA[:, 0:48],
                        deps=[tcolmm])
        BC_K, BC_V, BC_Q, BC_U, BC_G, BC_B = 0, 8, 16, 24, 32, 40

        tgb = g_gate
        tgwT = []
        prev_ev = [tbcols, tbcols]
        for mt in range(MT):
            ps = psTrA if mt % 2 == 0 else psTrB
            tt = p.emit("pe", "transpose", ps[:, 0:E],
                        gate_b[0:E, 128 * mt:128 * (mt + 1)],
                        identb[0:E, 0:E], deps=[tgb, tid, prev_ev[mt % 2]])
            tev = p.emit("dve", "tensor_copy", out=gwT[:, mt * E:(mt + 1) * E],
                         in_=ps[:, 0:E], deps=[tt])
            tgwT.append(tev)
            prev_ev[mt % 2] = tev

        xn_toks = [None] * NT
        xn_t_toks = {}
        xgbufs = [xg[:], logits[:, 0:D]]
        prev_xg_dve = [None, None]
        prev_xn = [None, None]
        g_x1p = [p.dma_group("g_x1a"), p.dma_group("g_x1b")]
        for jt in range(NT):
            xgb = xgbufs[jt % 2]
            so = 16 * (jt % 2)
            gx = p.dma("sp", out=xgb, in_=x_own[128 * jt:128 * (jt + 1), :],
                       deps=[prev_xn[jt % 2], prev_xg_dve[jt % 2]],
                       group=g_x1p[jt % 2])
            ts1 = p.emit("dve", "tensor_reduce", sml[:, so + 0:so + 1], xgb,
                         AX.X, ALU.add,
                         deps=[gx, xn_toks[jt - 2] if jt >= 2 else None])
            ts2 = p.emit("act", "activation", out=small4[:, (jt % 2) * 512:
                                                         (jt % 2) * 512 + 512],
                         in_=xgb[:, 0:512], func=AF.Square,
                         accum_out=sml[:, so + 1:so + 2],
                         deps=[gx, xn_toks[jt - 2] if jt >= 2 else None])
            ts2b = p.emit("act", "activation", out=small4[:, (jt % 2) * 512:
                                                          (jt % 2) * 512 + 512],
                          in_=xgb[:, 512:1024], func=AF.Square,
                          accum_out=sml[:, so + 2:so + 3], deps=[ts2])
            tm = p.emit("dve", "tensor_scalar_mul", sml[:, so + 3:so + 4],
                        sml[:, so + 0:so + 1], 1.0 / D, deps=[ts1])
            tss = p.emit("dve", "tensor_add", sml[:, so + 1:so + 2],
                         sml[:, so + 1:so + 2], sml[:, so + 2:so + 3],
                         deps=[ts2b])
            tmsq = p.emit("dve", "tensor_mul", sml[:, so + 4:so + 5],
                          sml[:, so + 3:so + 4], sml[:, so + 3:so + 4],
                          deps=[tm])
            tvar = p.emit("dve", "tensor_scalar_mul", sml[:, so + 5:so + 6],
                          sml[:, so + 1:so + 2], 1.0 / D, deps=[tss])
            tvar2 = p.emit("dve", "tensor_sub", sml[:, so + 5:so + 6],
                           sml[:, so + 5:so + 6], sml[:, so + 4:so + 5],
                           deps=[tvar, tmsq])
            tln = p.emit("act", "activation", out=sml[:, so + 6:so + 7],
                         in_=sml[:, so + 5:so + 6],
                         func=AF.Ln, bias=sml[:, 15:16], scale=1.0,
                         deps=[tvar2, teps])
            tlnh = p.emit("dve", "tensor_scalar_mul", sml[:, so + 6:so + 7],
                          sml[:, so + 6:so + 7], -0.5, deps=[tln])
            trstd = p.emit("act", "activation", out=sml[:, so + 7:so + 8],
                           in_=sml[:, so + 6:so + 7],
                           func=AF.Exp, bias=0.0, scale=1.0, deps=[tlnh])
            txn = p.emit("dve", "tensor_scalar", big0[:, jt * D:(jt + 1) * D],
                         xgb, sml[:, so + 3:so + 4], sml[:, so + 7:so + 8],
                         op0=ALU.subtract, op1=ALU.mult, deps=[trstd, tm])
            prev_xg_dve[jt % 2] = txn
            prev_xn[jt % 2] = ts2b
            xn_toks[jt] = txn
            for mt in range(MT):
                ps = psTrA if mt % 2 == 0 else psTrB
                ttr = p.emit("pe", "transpose", ps[:, 0:128],
                             big0[:, jt * D + 128 * mt: jt * D + 128 * (mt + 1)],
                             identb[:], deps=[txn, prev_ev[mt % 2]])
                tev = p.emit("dve", "tensor_copy",
                             out=big1[:, mt * TPC + jt * 128:
                                      mt * TPC + (jt + 1) * 128],
                             in_=ps[:, 0:128], deps=[ttr])
                xn_t_toks[(jt, mt)] = tev
                prev_ev[mt % 2] = tev
        xnT_done = p.bar("dve")

        for e in range(EPC):
            p.dma("pool", out=wgbf_d[e], in_=wg8[e],
                  deps=[g_x1p[0], g_x1p[1]], group=g_expc_e[e])
            p.dma("pool", out=wubf_d[e], in_=wu8[e],
                  deps=[g_x1p[0], g_x1p[1]], group=g_expc_e[e])
            p.dma("pool", out=wdbf_d[e], in_=wd8[e],
                  deps=[g_x1p[0], g_x1p[1]], group=g_expc_e[e])
            p.dma("pool", out=expc_flag_d[e:e + 1, 0:1],
                  in_=wg8[e][0:1, 0:1],
                  deps=[g_expc_e[e]], group=g_expc)
        zrows = c.SLOTS // 128
        for i in range(0, zrows, 16):
            nb = min(16, zrows - i)
            p.dma("act",
                  out=disp_part.rearrange("(r p) d -> p r d", p=128)[:, i:i + nb, :],
                  in_=arena_b[:, 0:nb * D].rearrange("p (r d) -> p r d", d=D),
                  deps=[tz0, g_x1p[0], g_x1p[1]], group=g_zero)

        g_xnst = p.dma_group("g_xnst")
        for mt in range(MT):
            p.dma("sp", out=xnT_mine_d[128 * mt:128 * (mt + 1), :],
                  in_=big1[:, mt * TPC:(mt + 1) * TPC],
                  deps=[xnT_done], group=g_xnst)
        t_ag_xnt = p.collective("AllGather", ALU.bypass, grp_batch,
                                xnT_mine_d, xnT_all_d, deps=[g_xnst])

        bc_state = {}

        def setup_bcreg(engine):
            reg = nc.alloc_register(mybir.EngineType.Pool, "bcreg")
            engine.reg_mov(reg, c.SLOTS - 1)
            bc_state["reg"] = reg
        p.thunks["pool"].append(setup_bcreg)
        g_scat = p.dma_group("g_scat")
        lg_toks = [None] * NT
        for jt in range(NT):
            ps = psA if jt % 2 == 0 else psB
            tmm = None
            for mt in range(MT):
                tmm = p.emit("pe", "matmul", ps[:, 0:E],
                             big1[:, mt * TPC + jt * 128:
                                  mt * TPC + (jt + 1) * 128],
                             gwT[:, mt * E:(mt + 1) * E],
                             start=(mt == 0), stop=(mt == MT - 1),
                             deps=[xn_t_toks[(jt, mt)], tgwT[mt], tbcols,
                                   prev_ev[0] if jt < 2 else None,
                                   prev_ev[1] if jt < 2 else None,
                                   lg_toks[jt - 2] if jt >= 2 else None])
            lg_toks[jt] = p.emit("dve", "tensor_copy",
                                 out=logits[:, jt * E:(jt + 1) * E],
                                 in_=ps[:, 0:E], deps=[tmm])
        A_toks = [None] * NT
        w8_toks = [None] * NT
        idx_toks = [None] * NT
        pos_toks = [None] * NT
        moe_steps = []
        rst = {"tsc": None, "tcs": None, "prevP": None, "tidx": None,
               "t_rs_disp": None}

        def step_top8(jt):
            sl = slice(jt * E, (jt + 1) * E)
            tmax = p.emit("dve", "max", out=t8[:], in_=logits[:, sl],
                          deps=[lg_toks[jt]])
            tmi = p.emit("dve", "max_index", out=idxu[:], in_max=t8[:],
                         in_values=logits[:, sl], deps=[tmax])
            tixf = p.emit("dve", "tensor_copy",
                          out=idxf[:, jt * 8:(jt + 1) * 8],
                          in_=idxu[:], deps=[tmi])
            idx_toks[jt] = tixf
            tneg = p.emit("dve", "tensor_scalar_mul", sml[:, 9:10], t8[:, 0:1],
                          -1.0, deps=[tmax])
            texp = p.emit("act", "activation", out=t8[:], in_=t8[:],
                          func=AF.Exp, bias=sml[:, 9:10], scale=1.0,
                          deps=[tneg, tmi])
            tsw = p.emit("dve", "tensor_reduce", sml[:, 10:11], t8[:], AX.X,
                         ALU.add, deps=[texp])
            twi = p.emit("dve", "reciprocal", sml[:, 11:12], sml[:, 10:11],
                         deps=[tsw])
            w8_toks[jt] = p.emit("dve", "tensor_single_scalar",
                                 w8[:, jt * 8:(jt + 1) * 8], t8[:],
                                 sml[:, 11:12], ALU.mult, deps=[twi])
            ta = None
            for k in range(8):
                te = p.emit("dve", "tensor_single_scalar", eq_bf[:],
                            iota64r[:], idxf[:, jt * 8 + k: jt * 8 + k + 1],
                            ALU.is_equal, deps=[tixf, ti3, ta])
                if k == 0:
                    ta = p.emit("dve", "tensor_copy",
                                out=A_sb[:, sl], in_=eq_bf[:], deps=[te])
                else:
                    ta = p.emit("dve", "tensor_add", A_sb[:, sl], A_sb[:, sl],
                                eq_bf[:], deps=[te])
            A_toks[jt] = ta

        def step_colsumS(jt):
            rst["tsc"] = p.emit("pe", "matmul", psB[0:E, jt:jt + 1],
                                A_sb[:, jt * E:(jt + 1) * E], ones_cb[:],
                                start=(jt == 0), stop=(jt == NT - 1),
                                deps=[A_toks[jt], tonesc])

        def step_slT_evac():
            rst["tslT"] = p.emit("dve", "tensor_copy", out=slT[:, 0:NT],
                                 in_=psB[0:E, 0:NT], deps=[rst["tsc"]])

        def step_colsumC(jt):
            rst["tcs"] = p.emit("pe", "matmul", psB[0:1, 0:E], ones_cb[:],
                                A_sb[:, jt * E:(jt + 1) * E],
                                start=(jt == 0), stop=(jt == NT - 1),
                                deps=[A_toks[jt], rst["tslT"]])

        def step_TRC():
            tcsr = p.emit("dve", "tensor_copy", out=cs_sb[:],
                          in_=psB[0:1, 0:E], deps=[rst["tcs"]])
            tT = p.emit("pe", "matmul", psB[0:8, 0:E], rmask[:], cs_sb[:],
                        start=True, stop=True, deps=[tcsr, g_rows])
            tTe = p.emit("dve", "tensor_copy", out=T_sb[:], in_=psB[0:8, 0:E],
                         deps=[tT])
            g_T = p.dma("sp", out=T_d, in_=T_sb[:], deps=[tTe])
            rst["t_rs_c"] = p.collective("ReduceScatter", ALU.add, grp_all,
                                         T_d, C_d, deps=[g_T])
            rst["tTe"] = tTe
            rst["tcsr"] = tcsr

        def step_R():
            g_C = p.dma("sp", out=C_sb[:], in_=C_d, deps=[rst["t_rs_c"]])
            tslt = p.emit("pe", "transpose", psB[0:NT, 0:E], slT[:, 0:NT],
                          identf[0:E, 0:E],
                          deps=[rst["tslT"], rst["tTe"], tidf])
            tsle = p.emit("dve", "tensor_copy", out=sl_sb[:],
                          in_=psB[0:NT, 0:E], deps=[tslt])
            tR1 = p.emit("pe", "matmul", psB[0:NT, 0:E], triu16[0:NT, 0:NT],
                         sl_sb[:], start=True, stop=False,
                         deps=[tsle, ttri16, rst["tcsr"]])
            tR2 = p.emit("pe", "matmul", psB[0:NT, 0:E], ones_row[0:1, 0:NT],
                         C_sb[:], start=False, stop=True, deps=[g_C])
            tRe = p.emit("dve", "tensor_copy", out=R_sb[:], in_=psB[0:NT, 0:E],
                         deps=[tR2])
            g_R = p.dma("sp", out=R_d, in_=R_sb[:], deps=[tRe])
            rst["g_Rr"] = p.dma("sp", out=Rrow[:],
                                in_=R_d.rearrange("(o a) b -> o (a b)", o=1),
                                deps=[g_R])
            rst["prevP"] = tRe

        def step_P(jt):
            tp1 = p.emit("pe", "matmul", psB[:, 0:E], triub[:],
                         A_sb[:, jt * E:(jt + 1) * E], start=True, stop=False,
                         deps=[A_toks[jt], ttri, rst["prevP"]])
            tp2 = p.emit("pe", "matmul", psB[:, 0:E], ones_row[:],
                         Rrow[0:1, jt * E:(jt + 1) * E], start=False,
                         stop=True, deps=[rst["g_Rr"]])
            tlast = None
            for k in range(8):
                te = p.emit("dve", "tensor_single_scalar", eq_bf[:],
                            iota64r[:], idxf[:, jt * 8 + k: jt * 8 + k + 1],
                            ALU.is_equal, deps=[idx_toks[jt], tlast])
                tmu = p.emit("dve", "tensor_tensor", out=scr64[:],
                             in0=psB[:, 0:E], in1=eq_bf[:], op=ALU.mult,
                             deps=[te, tp2])
                tlast = p.emit("dve", "tensor_reduce",
                               posal[:, jt * 8 + k: jt * 8 + k + 1],
                               scr64[:], AX.X, ALU.add, deps=[tmu])
            pos_toks[jt] = tlast
            rst["prevP"] = tlast

        def step_idx():
            NK = NT * 8
            tval = p.emit("dve", "tensor_scalar", validm[:, 0:NK],
                          posal[:, 0:NK], float(CAP), None, op0=ALU.is_lt,
                          deps=pos_toks)
            rst["twm"] = p.emit("dve", "tensor_mul", w8[:, 0:NK], w8[:, 0:NK],
                                validm[:, 0:NK], deps=[tval] + w8_toks)
            tidx1 = p.emit("dve", "tensor_scalar", idxf[:, 0:NK],
                           idxf[:, 0:NK], float(CAP), None, op0=ALU.mult,
                           deps=idx_toks)
            tidx2 = p.emit("dve", "tensor_add", idxf[:, 0:NK], idxf[:, 0:NK],
                           posal[:, 0:NK], deps=[tidx1] + pos_toks)
            tidx3 = p.emit("dve", "tensor_scalar", idxf[:, 0:NK],
                           idxf[:, 0:NK], -9999999.0, None, op0=ALU.add,
                           deps=[tidx2])
            tidx4 = p.emit("dve", "tensor_mul", idxf[:, 0:NK], idxf[:, 0:NK],
                           validm[:, 0:NK], deps=[tidx3, tval])
            tidx5 = p.emit("dve", "tensor_scalar", idxf[:, 0:NK],
                           idxf[:, 0:NK], 9999999.0, None, op0=ALU.add,
                           deps=[tidx4])
            rst["tidx"] = p.emit("dve", "tensor_copy", out=idx32[:, 0:NK],
                                 in_=idxf[:, 0:NK], deps=[tidx5])

        def step_scat(jt):
            for k in range(8):
                p.indirect_dma_breg(
                    bc_state,
                    out=disp_part,
                    out_offset=IndirectOffsetOnAxis(
                        idx32[:, jt * 8 + k: jt * 8 + k + 1], 0),
                    in_=big0[:, jt * D:(jt + 1) * D], in_offset=None,
                    oob_is_err=False,
                    deps=[rst["tidx"], xn_toks[jt], g_zero, g_expc],
                    group=g_scat)

        def step_rs_disp():
            rst["t_rs_disp"] = p.collective(
                "ReduceScatter", ALU.add, grp_all, disp_part, disp_own,
                deps=[g_scat])

        for jt in range(NT):
            moe_steps.append(lambda jt=jt: step_top8(jt))
        for jt in range(NT):
            moe_steps.append(lambda jt=jt: step_colsumS(jt))
        moe_steps.append(step_slT_evac)
        for jt in range(NT):
            moe_steps.append(lambda jt=jt: step_colsumC(jt))
        moe_steps.append(step_TRC)
        moe_steps.append(step_R)
        for jt in range(NT):
            moe_steps.append(lambda jt=jt: step_P(jt))
        moe_steps.append(step_idx)
        for jt in range(NT):
            moe_steps.append(lambda jt=jt: step_scat(jt))
        moe_steps.append(step_rs_disp)
        mstate = {"i": 0}

        def drain_moe(n):
            while n > 0 and mstate["i"] < len(moe_steps):
                moe_steps[mstate["i"]]()
                mstate["i"] += 1
                n -= 1

        n_pts = 2 * NQC * MT
        ph1 = 3 * NT + 1
        p1pts = max(1, (n_pts * 27) // 100)
        gappts = max(1, (n_pts * 12) // 100)
        plan = []
        cum = 0
        c1 = (ph1 + p1pts - 1) // p1pts
        for i in range(p1pts):
            take = min(c1, ph1 - cum)
            plan.append(take)
            cum += take
        for i in range(gappts):
            plan.append(0)
        n_steps_proj = len(moe_steps) - NT - 1
        rem = n_steps_proj - cum
        rem_pts = max(1, n_pts - len(plan))
        c2 = (rem + rem_pts - 1) // rem_pts
        for i in range(rem_pts):
            plan.append(c2)
        drain_pt = {"i": 0}

        def proj_hook():
            i = drain_pt["i"]
            drain_pt["i"] += 1
            n = plan[i] if i < len(plan) else 0
            drain_moe(min(n, n_steps_proj - mstate["i"]))

        def proj_to_dram(name, bc_base, dest_d, wcols_base, prev_stores=(None, None)):
            gw = p.dma_group(f"g_w{name}")
            p.dma("pool", out=wslot[:].rearrange("p (mt k) -> p mt k", k=D),
                  in_=w1[:, wcols_base:wcols_base + D]
                  .rearrange("(mt p) k -> p mt k", p=128),
                  deps=[p.bar("pe")], group=gw)
            gst_par = [p.dma_group(f"g_st{name}0"), p.dma_group(f"g_st{name}1")]
            ab = p.bar("act")
            silu_toks = [ab] * 4
            store_snaps = list(prev_stores)
            for ch in range(NQC):
                for ct in range(MT):
                    slot = ct % 4
                    ps = psP2 if slot < 2 else psQ2
                    pview = ps[:, (slot % 2) * 512: (slot % 2) * 512 + QCH]
                    tmm = None
                    for mt in range(MT):
                        tmm = p.emit(
                            "pe", "matmul", pview,
                            wslot[:, mt * D + ct * 128: mt * D + (ct + 1) * 128],
                            big1[:, mt * TPC + ch * QCH:
                                 mt * TPC + (ch + 1) * QCH],
                            start=(mt == 0), stop=(mt == MT - 1),
                            deps=[gw, xnT_done, silu_toks[slot]])
                    buf = (ct % 2) * QCH
                    tsil = p.emit("act", "activation",
                                  out=small4[:, buf:buf + QCH],
                                  in_=pview, func=AF.Silu,
                                  bias=bcols[:, bc_base + ct:bc_base + ct + 1],
                                  scale=1.0,
                                  deps=[tmm, tbcols, store_snaps[ct % 2]])
                    silu_toks[slot] = tsil
                    p.dma("sp", out=dest_d[128 * ct:128 * (ct + 1),
                                           ch * QCH:(ch + 1) * QCH],
                          in_=small4[:, buf:buf + QCH], deps=[tsil],
                          group=gst_par[ct % 2])
                    store_snaps[ct % 2] = gst_par[ct % 2].snap()
                    proj_hook()
            return gst_par

        g_q2 = proj_to_dram("q", BC_Q, qT_d, 2 * D)
        g_u2 = proj_to_dram("u", BC_U, uT_d, 0,
                            prev_stores=(g_q2[0].snap(), g_q2[1].snap()))
        g_q = g_q2[0]
        g_qb = g_q2[1]
        g_u = g_u2[0]
        g_ub = g_u2[1]
        drain_moe(n_steps_proj - mstate["i"])

        WG_O, WU_O, WD_O = 0, IT * 128 * 8, 0
        NDC = D // 512 if D >= 512 else 1
        DCH = min(512, D)
        ICH = [(i * 512, min(INTER, (i + 1) * 512))
               for i in range((INTER + 511) // 512)]
        g_dispT = p.dma_group("g_dispT")
        g_ew_par = [p.dma_group("g_ewA"), p.dma_group("g_ewB")]
        g_ew = [g_ew_par[e % 2] for e in range(EPC)]
        g_yst = p.dma_group("g_yst")
        exp_pe_done = [None] * EPC
        comb = {"i": 0, "last_par": [None, None], "last_tacc": None,
                "jt_done": [None] * NT,
                "grp": [p.dma_group("g_comb0"), p.dma_group("g_comb1")]}
        exs = {}

        def emit_dispT(pe_tok):
            big1_free = [pe_tok, xnT_done, g_xnst]
            for ht in range(MT):
                p.dma("act", out=big1[:, ht * TPC:(ht + 1) * TPC],
                      in_=disp_own[:, 128 * ht:128 * (ht + 1)], transpose=True,
                      deps=[rst["t_rs_disp"]] + big1_free, group=g_dispT)

        def emit_expert_phase():
            bar0_pe = p.bar("pe")
            bar0_act = p.bar("act")
            bar0_dve = p.bar("dve")
            prev_tsg = bar0_act
            prev_tgu = bar0_act
            y_ev_toks = []
            y_store_snap = None
            exp_gu_done = [None] * EPC
            for e in range(EPC):
                ar = arena_a
                deps_w = [g_expc, bar0_pe]
                if e >= 1:
                    deps_w.append(exp_gu_done[e - 1])
                wd_off = (e % 2) * IT * D
                deps_wd = deps_w + [g_zero]
                if e >= 2:
                    deps_wd.append(exp_pe_done[e - 2])
                for ht in range(MT):
                    p.dma("act",
                          out=ar[:, WG_O + ht * INTER: WG_O + (ht + 1) * INTER],
                          in_=wgbf_d[e][:, 128 * ht:128 * (ht + 1)],
                          transpose=True, deps=deps_w, group=g_ew[e])
                    p.dma("act",
                          out=ar[:, WU_O + ht * INTER: WU_O + (ht + 1) * INTER],
                          in_=wubf_d[e][:, 128 * ht:128 * (ht + 1)],
                          transpose=True, deps=deps_w, group=g_ew[e])
                for it in range(IT):
                    p.dma("act",
                          out=arena_b[:, wd_off + it * D:
                                      wd_off + (it + 1) * D],
                          in_=wdbf_d[e][:, 128 * it:128 * (it + 1)],
                          transpose=True, deps=deps_wd,
                          group=g_ew[e])
                for st in range(CAP // 128 if CAP >= 128 else 1):
                    sbase = e * CAP + st * 128
                    srows = min(128, CAP)
                    tg = tu = None
                    for ht in range(MT):
                        lhs = big1[:, ht * TPC + sbase:
                                   ht * TPC + sbase + srows]
                        for (tgt, wo, is_g) in ((psP2, WG_O, True),
                                                (psQ2, WU_O, False)):
                            for (n0, n1) in ICH:
                                tt = p.emit(
                                    "pe", "matmul", tgt[0:srows, n0:n1], lhs,
                                    ar[:, wo + ht * INTER + n0:
                                       wo + ht * INTER + n1],
                                    start=(ht == 0), stop=(ht == MT - 1),
                                    deps=[g_dispT, g_ew[e],
                                          prev_tsg if is_g else prev_tgu])
                                if is_g:
                                    tg = tt
                                else:
                                    tu = tt
                    tsg = p.emit("act", "activation",
                                 out=small4[0:srows, 0:INTER],
                                 in_=psP2[0:srows, 0:INTER], func=AF.Silu,
                                 deps=[tg, prev_tgu])
                    prev_tsg = tsg
                    tgu = p.emit("dve", "tensor_tensor",
                                 out=small4[0:srows, INTER:2 * INTER],
                                 in0=small4[0:srows, 0:INTER],
                                 in1=psQ2[0:srows, 0:INTER], op=ALU.mult,
                                 deps=[tsg, tu])
                    prev_tgu = tgu
                    gut_toks = []
                    prev_tr = [bar0_dve, bar0_dve]
                    for it in range(IT):
                        pst = psTrA if it % 2 == 0 else psTrB
                        ttr = p.emit("pe", "transpose", pst[:, 0:srows],
                                     small4[0:srows, INTER + it * 128:
                                            INTER + (it + 1) * 128],
                                     identb[0:srows, 0:srows],
                                     deps=[tgu, prev_tr[it % 2]])
                        tev = p.emit("dve", "tensor_copy",
                                     out=small4[:, 2 * INTER + it * 128:
                                                2 * INTER + it * 128 + srows],
                                     in_=pst[:, 0:srows], deps=[ttr])
                        gut_toks.append(tev)
                        prev_tr[it % 2] = tev
                    ty = None
                    for it in range(IT):
                        for nd in range(NDC):
                            ty = p.emit(
                                "pe", "matmul",
                                psR2[0:srows, nd * DCH:(nd + 1) * DCH],
                                small4[:, 2 * INTER + it * 128:
                                       2 * INTER + it * 128 + srows],
                                arena_b[:, wd_off + it * D + nd * DCH:
                                        wd_off + it * D + (nd + 1) * DCH],
                                start=(it == 0), stop=(it == IT - 1),
                                deps=[gut_toks[it],
                                      y_ev_toks[-1] if y_ev_toks
                                      else bar0_act])
                    YO = 2 * INTER + IT * 128
                    tyev = p.emit("dve", "tensor_copy",
                                  out=small4[0:srows, YO:YO + D],
                                  in_=psR2[0:srows, 0:D],
                                  deps=[ty, y_store_snap])
                    y_ev_toks.append(tyev)
                    p.dma("sp", out=y_own_d[sbase:sbase + srows, :],
                          in_=small4[0:srows, YO:YO + D], deps=[tyev],
                          group=g_yst)
                    y_store_snap = g_yst.snap()
                    exp_gu_done[e] = tu
                exp_pe_done[e] = p.bar("pe")
            exs["t_ag_y"] = p.collective("AllGather", ALU.bypass, grp_all,
                                         y_own_d, y_all_d, deps=[g_yst])
            tmz = p.emit("dve", "memset", big0[:], 0.0,
                         deps=[g_scat, p.bar("pool")])
            exs["tmz"] = tmz
            comb["last_par"] = [tmz, tmz]
            comb["last_tacc"] = tmz

        def emit_combine(n):
            for _ in range(n):
                i = comb["i"]
                if i >= NT * 8:
                    return
                comb["i"] += 1
                jt, k = i // 8, i % 8
                par = i % 2
                yg = y_g0 if par == 0 else y_g1
                g = p.indirect_dma_breg(
                    bc_state,
                    out=yg[:], out_offset=None,
                    in_=y_all_d,
                    in_offset=IndirectOffsetOnAxis(
                        idx32[:, jt * 8 + k: jt * 8 + k + 1], 0),
                    oob_is_err=False,
                    deps=[exs["t_ag_y"], rst["tidx"], comb["last_par"][par],
                          tyg0, tyg1],
                    group=comb["grp"][par])
                tacc = p.emit("dve", "scalar_tensor_tensor",
                              out=big0[:, jt * D:(jt + 1) * D],
                              in0=yg[:],
                              scalar=w8[:, jt * 8 + k: jt * 8 + k + 1],
                              in1=big0[:, jt * D:(jt + 1) * D],
                              op0=ALU.mult, op1=ALU.add,
                              deps=[g, rst["twm"], exs["tmz"],
                                    comb["last_tacc"]])
                comb["last_par"][par] = tacc
                comb["last_tacc"] = tacc
                if k == 7:
                    comb["jt_done"][jt] = tacc

        N_AT = 2 * 2 * NQC
        comb_per = (NT * 8 + N_AT - 1) // N_AT
        n_steps = len(moe_steps)
        avT_ev = []
        g_avst = [p.dma_group(f"g_avst{i}") for i in range(4)]
        ablk = {"i": 0}
        g_qtg_grp = p.dma_group("g_qtg")
        glx_par = [p.dma_group(f"g_glx{i}") for i in range(8)]
        for g in range(4):
            if g == 1:
                drain_moe(n_steps)
                emit_expert_phase()
            bar_pe = p.bar("pe")
            bar_act = p.bar("act")
            bar_dve = p.bar("dve")
            gw = p.dma_group(f"g_kv{g}")
            p.dma("pool",
                  out=wslot[:].rearrange("p (mt k) -> p mt k", k=D)[:, :, 0:256],
                  in_=w1[:, 3 * D + 256 * g: 3 * D + 256 * (g + 1)]
                  .rearrange("(mt p) k -> p mt k", p=128),
                  deps=[bar_pe], group=gw)
            p.dma("pool",
                  out=wslot[:].rearrange("p (mt k) -> p mt k", k=D)[:, :, 256:512],
                  in_=w1[:, 1 * D + 256 * g: 1 * D + 256 * (g + 1)]
                  .rearrange("(mt p) k -> p mt k", p=128),
                  deps=[bar_pe], group=gw)

            PAR_N = min(8, (2 * TPC) // GCH)
            NCB = S // GCH
            arena_deps = [bar_act, bar_dve, g_zero, g_u, g_ub,
                          g_q, g_qb] \
                if g == 0 else \
                ([exp_pe_done[EPC - 1], bar_act, bar_dve, g_yst]
                 if g == 1 else [bar_act, bar_dve])
            pe_mt = [bar_pe] * PAR_N
            kv_bank_k = [gw, gw]
            kv_bank_v = [[gw, gw], [gw, gw]]
            sm_free = [[bar_pe, bar_pe], [bar_pe, bar_pe]]
            prev_tv = [bar_dve, bar_dve]
            pend_v = []
            CHQ = ["sp", "pool", "act"]

            def flush_vev():
                while pend_v:
                    tsil, cbq, ctq = pend_v.pop(0)
                    sbase = ((cbq % 2) * 2 + ctq) * GCH
                    ttr = None
                    for sub in range(GCH // 128):
                        kt = (cbq * GCH + sub * 128) // 128
                        pst = psTrA if sub % 2 == 0 else psTrB
                        ttr = p.emit(
                            "pe", "transpose", pst[:, 0:128],
                            small4[:, sbase + sub * 128:
                                   sbase + (sub + 1) * 128],
                            identb[:], deps=[tsil, prev_tv[sub % 2]])
                        tve = p.emit(
                            "dve", "tensor_copy",
                            out=arena_b[:, kt * 256 + ctq * 128:
                                        kt * 256 + (ctq + 1) * 128],
                            in_=pst[:, 0:128], deps=[ttr])
                        prev_tv[sub % 2] = tve
                    sm_free[cbq % 2][ctq] = ttr

            for cb in range(NCB):
                rb = (cb * GCH) // TPC
                tloc = cb * GCH - rb * TPC
                tlast = None
                for mt in range(MT):
                    par = (cb * MT + mt) % PAR_N
                    p.dma(CHQ[par % 3],
                          out=qtg[:, par * GCH:(par + 1) * GCH],
                          in_=xnT_all_d[rb * D + 128 * mt:
                                        rb * D + 128 * (mt + 1),
                                        tloc:tloc + GCH],
                          deps=[t_ag_xnt, pe_mt[par], bar_act, g_qtg_grp],
                          group=glx_par[par])
                    for pas in range(4):
                        is_v = pas >= 2
                        ct = pas % 2
                        col_off = (256 if is_v else 0) + ct * 128
                        psv = psQ2 if cb % 2 == 0 else psR2
                        dst = (psP2 if not is_v else psv)[
                            :, ct * 512: ct * 512 + GCH]
                        bank_dep = (kv_bank_k[ct] if not is_v
                                    else kv_bank_v[cb % 2][ct])
                        tlast = p.emit(
                            "pe", "matmul", dst,
                            wslot[:, mt * D + col_off:
                                  mt * D + col_off + 128],
                            qtg[:, par * GCH:(par + 1) * GCH],
                            start=(mt == 0), stop=(mt == MT - 1),
                            deps=[gw, glx_par[par], bank_dep] + arena_deps)
                    pe_mt[par] = tlast
                flush_vev()
                for ct in range(2):
                    t = p.emit("act", "activation",
                               out=arena_a[:, ct * S + cb * GCH:
                                           ct * S + (cb + 1) * GCH],
                               in_=psP2[:, ct * 512: ct * 512 + GCH],
                               func=AF.Silu,
                               bias=bcols[:, BC_K + 2 * g + ct:
                                          BC_K + 2 * g + ct + 1],
                               scale=1.0, deps=[tlast, tbcols])
                    kv_bank_k[ct] = t
                for ct in range(2):
                    psv = psQ2 if cb % 2 == 0 else psR2
                    t = p.emit("act", "activation",
                               out=small4[:, ((cb % 2) * 2 + ct) * GCH:
                                          ((cb % 2) * 2 + ct + 1) * GCH],
                               in_=psv[:, ct * 512: ct * 512 + GCH],
                               func=AF.Silu,
                               bias=bcols[:, BC_V + 2 * g + ct:
                                          BC_V + 2 * g + ct + 1],
                               scale=1.0,
                               deps=[tlast, tbcols, sm_free[cb % 2][ct]])
                    kv_bank_v[cb % 2][ct] = t
                    pend_v.append((t, cb, ct))
            flush_vev()
            kT_ready = p.bar("act")
            v_ready = p.bar("dve")
            kv_pe = p.bar("pe")
            silu_t = [None, None, None]
            av_t = [None, None, None]
            g_qt = p.dma("sp", out=qtg[:].rearrange("p (hp t) -> p hp t", hp=2),
                         in_=qT_d[256 * g:256 * (g + 1), :]
                         .rearrange("(hp pp) t -> pp hp t", pp=128),
                         deps=[g_q, g_qb, p.bar("pe")], group=g_qtg_grp)
            big1_pe_tok = p.bar("pe")
            scps = [psP2, psQ2, psR2]
            for hp in range(2):
                for qc in range(NQC):
                    q0 = qc * QCH

                    def sc_mm(kt):
                        scp = scps[kt % 3]
                        p.emit("pe", "matmul", scp[:, 0:QCH],
                               arena_a[0:64, hp * S + kt * 128:
                                       hp * S + (kt + 1) * 128],
                               qtg[0:64, hp * TPC + q0: hp * TPC + q0 + QCH],
                               start=True, stop=True, tile_position=(0, 0),
                               deps=[kT_ready, g_qt, silu_t[kt % 3], kv_pe,
                                     v_ready],
                               track=False)
                        return p.emit("pe", "matmul", scp[:, 512:512 + QCH],
                                      arena_a[64:128, hp * S + kt * 128:
                                              hp * S + (kt + 1) * 128],
                                      qtg[64:128, hp * TPC + q0:
                                          hp * TPC + q0 + QCH],
                                      start=True, stop=True,
                                      tile_position=(64, 0))

                    def silu_mm(kt, t2):
                        scp = scps[kt % 3]
                        if QCH == 512:
                            return p.emit("act", "activation",
                                          out=small4[:, (kt % 3) * 1024:
                                                     (kt % 3) * 1024 + 1024],
                                          in_=scp[:, 0:1024], func=AF.Silu,
                                          deps=[t2, av_t[kt % 3]])
                        p.emit("act", "activation",
                               out=small4[:, (kt % 3) * 1024:
                                          (kt % 3) * 1024 + QCH],
                               in_=scp[:, 0:QCH], func=AF.Silu,
                               deps=[t2, av_t[kt % 3]])
                        return p.emit("act", "activation",
                                      out=small4[:, (kt % 3) * 1024 + 512:
                                                 (kt % 3) * 1024 + 512 + QCH],
                                      in_=scp[:, 512:512 + QCH],
                                      func=AF.Silu)

                    def av_open():
                        return p.emit("pe", "matmul", psA[:, 0:QCH],
                                      ones_row_bf[0:1, 0:128],
                                      zrow_bf[0:1, 0:QCH],
                                      start=True, stop=False,
                                      skip_group_check=True,
                                      deps=[tzrow, tones_bf, v_ready,
                                            avT_ev[-1] if avT_ev else None])

                    def av_mm(kt, tsil):
                        par = kt % 3
                        ta1 = p.emit("pe", "matmul", psA[0:64, 0:QCH],
                                     arena_b[:, kt * 256 + hp * 128:
                                             kt * 256 + hp * 128 + 64],
                                     small4[:, par * 1024: par * 1024 + QCH],
                                     start=False, stop=(kt == KT - 1),
                                     tile_position=(0, 0),
                                     skip_group_check=True,
                                     deps=[tsil, v_ready])
                        ta2 = p.emit("pe", "matmul", psA[64:128, 0:QCH],
                                     arena_b[:, kt * 256 + hp * 128 + 64:
                                             kt * 256 + hp * 128 + 128],
                                     small4[:, par * 1024 + 512:
                                            par * 1024 + 512 + QCH],
                                     start=False, stop=(kt == KT - 1),
                                     tile_position=(0, 64),
                                     skip_group_check=True)
                        av_t[par] = ta2
                        return ta1, ta2

                    sils = {}
                    av_open()
                    for kt in range(min(2, KT)):
                        t2 = sc_mm(kt)
                        sils[kt] = silu_mm(kt, t2)
                        silu_t[kt % 3] = sils[kt]
                    for kt in range(2, KT):
                        t2 = sc_mm(kt)
                        av_mm(kt - 2, sils.pop(kt - 2))
                        sils[kt] = silu_mm(kt, t2)
                        silu_t[kt % 3] = sils[kt]
                    for kt in range(max(0, KT - 2), KT):
                        tav1, tav = av_mm(kt, sils.pop(kt))
                    apar = ablk["i"] % 4
                    ablk["i"] += 1
                    tev = p.emit("dve", "tensor_copy",
                                 out=avst[:, apar * 512: apar * 512 + QCH],
                                 in_=psA[:, 0:QCH],
                                 deps=[tav1, tav, g_avst[apar]])
                    p.dma("sp", out=avT_d[(2 * g + hp) * 128:
                                          (2 * g + hp) * 128 + 128,
                                          q0:q0 + QCH],
                          in_=avst[:, apar * 512: apar * 512 + QCH],
                          deps=[tev], group=g_avst[apar])
                    avT_ev.append(tev)
                    if g == 0:
                        drain_moe(3)
                    elif g >= 2:
                        emit_combine(comb_per)
            if g == 0:
                drain_moe(n_steps)
                emit_dispT(big1_pe_tok)
        emit_combine(NT * 8)
        att_pe = p.bar("pe")
        att_act = p.bar("act")

        gw2 = p.dma_group("g_w2")
        w2_fold = [None] * MT
        tb2p = None
        tcmm = None
        prev_xg_pe = None
        for mt in range(MT):
            gl = p.dma("sp", out=xg[:], in_=w2[128 * mt:128 * (mt + 1), :],
                       deps=[att_pe, att_act, w2_fold[mt - 1] if mt else None,
                             prev_xg_pe,
                             comb["last_tacc"] if mt == 0 else None],
                       group=gw2)
            for nd in range(NDC):
                tb2p = p.emit("pe", "matmul",
                              psP2[0:1, nd * DCH:(nd + 1) * DCH],
                              bcols[:, BC_B + mt: BC_B + mt + 1],
                              xg[:, nd * DCH:(nd + 1) * DCH],
                              start=(mt == 0), stop=(mt == MT - 1),
                              deps=[gw2, tbcols, att_pe])
            tfold = p.emit("dve", "tensor_single_scalar",
                           wslot[:, mt * D:(mt + 1) * D], xg[:],
                           bcols[:, BC_G + mt: BC_G + mt + 1], ALU.mult,
                           deps=[gw2, tbcols])
            w2_fold[mt] = tfold
            prev_xg_pe = tb2p
            for nd in range(NDC):
                tcmm = p.emit("pe", "matmul",
                              psQ2[0:1, nd * DCH:(nd + 1) * DCH],
                              ones_cb[:],
                              wslot[:, mt * D + nd * DCH:
                                    mt * D + (nd + 1) * DCH],
                              start=(mt == 0), stop=(mt == MT - 1),
                              deps=[tfold, tonesc])
        g_b2l = p.dma("sp", out=row_stage[:],
                      in_=b2.rearrange("(o n) -> o n", o=1),
                      deps=[prev_mm, g_rst])
        tb2pr = p.emit("dve", "tensor_add", b2prow[:], psP2[0:1, 0:D],
                       row_stage[:], deps=[tb2p, g_b2l, w2_fold[MT - 1]])
        tcrow = p.emit("dve", "tensor_copy", out=crow[:], in_=psQ2[0:1, 0:D],
                       deps=[tcmm])
        tbc1 = None
        for nd in range(NDC):
            tbc1 = p.emit("pe", "matmul", psP2[:, nd * DCH:(nd + 1) * DCH],
                          ones_row[:], b2prow[0:1, nd * DCH:(nd + 1) * DCH],
                          start=True, stop=True,
                          deps=[tb2pr, tcrow])
        tb2rep = p.emit("dve", "tensor_copy", out=b2p_rep[:], in_=psP2[:, 0:D],
                        deps=[tbc1])
        tbc2 = None
        for nd in range(NDC):
            tbc2 = p.emit("pe", "matmul", psQ2[:, nd * DCH:(nd + 1) * DCH],
                          ones_row[:], crow[0:1, nd * DCH:(nd + 1) * DCH],
                          start=True, stop=True,
                          deps=[tcrow, tb2rep])
        tcrep = p.emit("dve", "tensor_copy", out=c_rep[:], in_=psQ2[:, 0:D],
                       deps=[tbc2])

        INV2 = 1.0 / float(S)
        g_out = p.dma_group("g_out")
        prev_rd = [{"st": tcrep, "y": tb2rep}, {"st": tcrep, "y": tcrep}]
        prev_ujt = [tcrep, tcrep]
        gul_par = [p.dma_group("g_ul0"), p.dma_group("g_ul1")]
        g_xre_par = [p.dma_group("g_xrea"), p.dma_group("g_xreb")]
        prev_epi = [tcrep, tcrep]
        xg_z = [xg[:], logits[:, 0:D]]
        ypss = [psP2, psQ2]
        for jt in range(NT):
            pp = jt % 2
            so = 8 * pp
            yps = ypss[pp]
            stA = psA[:, 0:1] if pp == 0 else psR2[:, 0:1]
            stB = psB[:, 0:1] if pp == 0 else psR2[:, 512:513]
            g_ul = gul_par[pp]
            p.dma("sp", out=ujt[:, pp * MT * 128:(pp + 1) * MT * 128]
                  .rearrange("p (mt t) -> p mt t", t=128),
                  in_=uT_d.rearrange("(mt p) t -> p mt t", p=128)[
                      :, :, jt * 128:(jt + 1) * 128],
                  deps=[g_u, g_ub, prev_ujt[pp]], group=g_ul)
            AVJ = 3072 + pp * 1024
            p.dma("sp", out=small4[:, AVJ:AVJ + 1024]
                  .rearrange("p (mt t) -> p mt t", t=128),
                  in_=avT_d.rearrange("(mt p) t -> p mt t", p=128)[
                      :, :, jt * 128:(jt + 1) * 128],
                  deps=[g_avst[0], g_avst[1], g_avst[2], g_avst[3],
                        prev_ujt[pp]], group=g_ul)
            g_xre = p.dma("sp", out=xg_z[pp],
                          in_=x_own[128 * jt:128 * (jt + 1), :],
                          deps=[prev_epi[pp], g_out], group=g_xre_par[pp])
            tzs = tss2 = tw2 = None
            zpar_tok = [None, None]
            last_z_dve = None
            for mt in range(MT):
                zp = mt % 2
                tz = p.emit("dve", "tensor_tensor",
                            out=zt[:, zp * 128:(zp + 1) * 128],
                            in0=small4[:, 3072 + pp * 1024 + mt * 128:
                                       3072 + pp * 1024 + (mt + 1) * 128],
                            in1=ujt[:, pp * MT * 128 + mt * 128:
                                    pp * MT * 128 + (mt + 1) * 128],
                            op=ALU.mult,
                            deps=[g_ul, avT_ev[-1], zpar_tok[zp]])
                tz2 = p.emit("dve", "tensor_tensor",
                             out=z2t[:, zp * 128:(zp + 1) * 128],
                             in0=zt[:, zp * 128:(zp + 1) * 128],
                             in1=zt[:, zp * 128:(zp + 1) * 128],
                             op=ALU.mult, deps=[tz])
                last_z_dve = tz2
                tzs = p.emit("pe", "matmul", stA,
                             zt[:, zp * 128:(zp + 1) * 128], ones_cb[:],
                             start=(mt == 0), stop=(mt == MT - 1),
                             deps=[tz, tonesc,
                                   prev_rd[pp]["st"] if mt == 0 else None])
                tss2 = p.emit("pe", "matmul", stB,
                              zt[:, zp * 128:(zp + 1) * 128] if False else
                              z2t[:, zp * 128:(zp + 1) * 128], ones_cb[:],
                              start=(mt == 0), stop=(mt == MT - 1),
                              deps=[tz2,
                                    prev_rd[pp]["st"] if mt == 0 else None])
                for nd in range(NDC):
                    tw2 = p.emit("pe", "matmul",
                                 yps[:, nd * DCH:(nd + 1) * DCH],
                                 zt[:, zp * 128:(zp + 1) * 128],
                                 wslot[:, mt * D + nd * DCH:
                                       mt * D + (nd + 1) * DCH],
                                 start=(mt == 0), stop=(mt == MT - 1),
                                 deps=[tz, w2_fold[mt],
                                       prev_rd[pp]["y"] if mt == 0 and nd == 0
                                       else None])
                zpar_tok[zp] = tw2
            t1 = p.emit("dve", "tensor_copy", out=statc[:, so + 0:so + 1],
                        in_=stA, deps=[tzs])
            t2 = p.emit("dve", "tensor_copy", out=statc[:, so + 1:so + 2],
                        in_=stB, deps=[tss2])
            t3 = p.emit("dve", "tensor_scalar_mul", statc[:, so + 2:so + 3],
                        statc[:, so + 0:so + 1], 1.0 / D, deps=[t1])
            t4 = p.emit("dve", "tensor_mul", statc[:, so + 3:so + 4],
                        statc[:, so + 2:so + 3], statc[:, so + 2:so + 3],
                        deps=[t3])
            t5 = p.emit("dve", "tensor_scalar_mul", statc[:, so + 4:so + 5],
                        statc[:, so + 1:so + 2], 1.0 / D, deps=[t2])
            t6 = p.emit("dve", "tensor_sub", statc[:, so + 4:so + 5],
                        statc[:, so + 4:so + 5], statc[:, so + 3:so + 4],
                        deps=[t5, t4])
            t6b = p.emit("dve", "tensor_scalar_mul", statc[:, so + 4:so + 5],
                         statc[:, so + 4:so + 5], float(INV2 * INV2),
                         deps=[t6])
            t7 = p.emit("act", "activation", out=statc[:, so + 5:so + 6],
                        in_=statc[:, so + 4:so + 5], func=AF.Ln,
                        bias=sml[:, 15:16], scale=1.0, deps=[t6b, teps])
            t7b = p.emit("dve", "tensor_scalar_mul", statc[:, so + 5:so + 6],
                         statc[:, so + 5:so + 6], -0.5, deps=[t7])
            t8b = p.emit("act", "activation", out=statc[:, so + 6:so + 7],
                         in_=statc[:, so + 5:so + 6], func=AF.Exp, bias=0.0,
                         scale=1.0, deps=[t7b])
            t9 = p.emit("dve", "tensor_scalar_mul", statc[:, so + 6:so + 7],
                        statc[:, so + 6:so + 7], float(INV2), deps=[t8b])
            t10 = p.emit("dve", "tensor_mul", statc[:, so + 7:so + 8],
                         statc[:, so + 2:so + 3], statc[:, so + 6:so + 7],
                         deps=[t9, t3])
            t11 = p.emit("dve", "tensor_scalar_mul", statc[:, so + 7:so + 8],
                         statc[:, so + 7:so + 8], -1.0, deps=[t10])
            last = None
            for nd in range(NDC):
                slx = slice(nd * DCH, (nd + 1) * DCH)
                o1 = p.emit("dve", "scalar_tensor_tensor", out=xg_z[pp][:, slx],
                            in0=yps[:, slx], scalar=statc[:, so + 6:so + 7],
                            in1=xg_z[pp][:, slx], op0=ALU.mult, op1=ALU.add,
                            deps=[tw2, t9, g_xre, last])
                o2 = p.emit("dve", "scalar_tensor_tensor", out=xg_z[pp][:, slx],
                            in0=c_rep[:, slx], scalar=statc[:, so + 7:so + 8],
                            in1=xg_z[pp][:, slx], op0=ALU.mult, op1=ALU.add,
                            deps=[o1, t11, tcrep])
                o3 = p.emit("dve", "tensor_add", xg_z[pp][:, slx],
                            xg_z[pp][:, slx], b2p_rep[:, slx],
                            deps=[o2, tb2rep])
                last = p.emit("dve", "tensor_add", xg_z[pp][:, slx],
                              xg_z[pp][:, slx],
                              big0[:, jt * D + nd * DCH:
                                   jt * D + (nd + 1) * DCH],
                              deps=[o3, comb["jt_done"][jt], exs["tmz"]])
            p.dma("sp", out=out[128 * jt:128 * (jt + 1), :], in_=xg_z[pp][:],
                  deps=[last], group=g_out)
            prev_rd[pp] = {"st": t2, "y": last}
            prev_ujt[pp] = last_z_dve
            prev_epi[pp] = last
        p.wait_raw("sp", g_out)
        p.build(block)
    return nc


_CFG = Cfg()
_NC_CACHE = {}


def _get_nc(cfg):
    key = (cfg.S, cfg.TPC, cfg.CAP, cfg.INTER)
    if key not in _NC_CACHE:
        _NC_CACHE[key] = build_program(cfg)
    return _NC_CACHE[key]


def make_in_maps(cfg, L0_attn_x, L0_attn_W1, L0_attn_b1, L0_attn_W2,
                 L0_attn_b2, L0_attn_gamma, L0_attn_beta, L0_ffn_gate_w,
                 L0_ffn_w_gate, L0_ffn_w_up, L0_ffn_w_down):
    c = cfg
    xflat = np.ascontiguousarray(
        np.asarray(L0_attn_x, dtype=np.float32).reshape(-1, c.D))
    w1 = np.ascontiguousarray(np.asarray(L0_attn_W1, dtype=np.float32))
    b1 = np.ascontiguousarray(np.asarray(L0_attn_b1, dtype=np.float32))
    w2 = np.ascontiguousarray(np.asarray(L0_attn_W2, dtype=np.float32))
    b2 = np.ascontiguousarray(np.asarray(L0_attn_b2, dtype=np.float32))
    gm = np.ascontiguousarray(np.asarray(L0_attn_gamma, dtype=np.float32))
    bt = np.ascontiguousarray(np.asarray(L0_attn_beta, dtype=np.float32))
    gw = np.ascontiguousarray(np.asarray(L0_ffn_gate_w, dtype=np.float32))
    wgt = np.asarray(L0_ffn_w_gate, dtype=np.float32)
    wup = np.asarray(L0_ffn_w_up, dtype=np.float32)
    wdn = np.asarray(L0_ffn_w_down, dtype=np.float32)
    in_maps = []
    for r in range(8):
        rm = (np.arange(8)[None, :] > r).astype(np.float32)
        in_maps.append({
            "x_own": np.ascontiguousarray(xflat[r * c.TPC:(r + 1) * c.TPC]),
            "w1": w1, "b1": b1, "w2": w2, "b2": b2,
            "gamma": gm, "beta": bt, "gate_w": gw,
            "wg8": np.ascontiguousarray(wgt[r * c.EPC:(r + 1) * c.EPC]),
            "wu8": np.ascontiguousarray(wup[r * c.EPC:(r + 1) * c.EPC]),
            "wd8": np.ascontiguousarray(wdn[r * c.EPC:(r + 1) * c.EPC]),
            "rankmask": np.ascontiguousarray(rm),
        })
    return in_maps


def kernel(**inputs):
    c = _CFG
    nc = _get_nc(c)
    in_maps = make_in_maps(c, **inputs)
    res = run_bass_kernel_spmd(nc, in_maps, core_ids=list(range(8)))
    shards = [np.asarray(res.results[r]["out"]) for r in range(8)]
    full = np.concatenate(shards, axis=0).reshape(2, c.S, c.D)
    return full.astype(np.float32)


# version 22
# speedup vs baseline: 1.0068x; 1.0068x over previous
import numpy as np
import ml_dtypes
from contextlib import ExitStack
import concourse.bass as bass
import concourse.mybir as mybir
from concourse.bass_utils import run_bass_kernel_spmd
from concourse.bass_core_rust import IndirectOffsetOnAxis

F32 = mybir.dt.float32
BF16 = mybir.dt.bfloat16
I32 = mybir.dt.int32
U32 = mybir.dt.uint32
AF = mybir.ActivationFunctionType
ALU = mybir.AluOpType
AX = mybir.AxisListType

ENGINES = ("pe", "act", "dve", "pool", "sp")


class Tok:
    __slots__ = ("sem", "val")

    def __init__(self, sem, val):
        self.sem = sem
        self.val = val


class DmaGroup:
    __slots__ = ("sem", "val")

    def __init__(self, sem):
        self.sem = sem
        self.val = 0

    def snap(self):
        return Tok(self.sem, self.val)


class Prog:
    def __init__(self, nc, ctx):
        self.nc = nc
        self.ctx = ctx
        self.sems = {e: ctx.enter_context(nc.semaphore("seq_" + e))
                     for e in ENGINES}
        self.counts = {e: 0 for e in ENGINES}
        self.thunks = {e: [] for e in ENGINES}
        self.watermark = {e: {} for e in ENGINES}
        self._group_n = 0

    def new_sem(self, name):
        return self.ctx.enter_context(self.nc.semaphore(name))

    def dma_group(self, name=None):
        self._group_n += 1
        sem = self.new_sem(name or f"dmag_{self._group_n}")
        return DmaGroup(sem)

    def _waits_for(self, eng, deps):
        wm = self.watermark[eng]
        needed = []
        for d in deps:
            if d is None:
                continue
            if isinstance(d, DmaGroup):
                d = d.snap()
            sem, val = d.sem, d.val
            key = id(sem)
            if val > wm.get(key, 0):
                wm[key] = val
                needed.append((sem, val))
        return needed

    def emit(self, eng, fn, *args, deps=(), track=True, **kwargs):
        my_sem = self.sems[eng]
        if track:
            self.counts[eng] += 1
        needed = self._waits_for(eng, deps)
        my_inc = 1 if track else 0

        def thunk(engine, needed=needed, fn=fn, args=args, kwargs=kwargs,
                  my_inc=my_inc, my_sem=my_sem):
            for (s, v) in needed:
                engine.wait_ge(s, v)
            inst = getattr(engine, fn)(*args, **kwargs)
            if my_inc:
                inst.then_inc(my_sem, 1)
            return inst

        self.thunks[eng].append(thunk)
        return Tok(my_sem, self.counts[eng]) if track else None

    def dma(self, eng, *args, deps=(), group=None, **kwargs):
        if group is None:
            group = self.dma_group()
        needed = self._waits_for(eng, deps)
        group.val += 16
        gsem = group.sem

        def thunk(engine, needed=needed, args=args, kwargs=kwargs, gsem=gsem):
            for (s, v) in needed:
                engine.wait_ge(s, v)
            engine.dma_start(*args, **kwargs).then_inc(gsem, 16)

        self.thunks[eng].append(thunk)
        return group

    def indirect_dma(self, *args, deps=(), group=None, **kwargs):
        if group is None:
            group = self.dma_group()
        needed = self._waits_for("pool", deps)
        group.val += 16
        gsem = group.sem

        def thunk(engine, needed=needed, args=args, kwargs=kwargs, gsem=gsem):
            for (s, v) in needed:
                engine.wait_ge(s, v)
            engine.indirect_dma_start(*args, **kwargs).then_inc(gsem, 16)

        self.thunks["pool"].append(thunk)
        return group

    def indirect_dma_breg(self, bc_state, *args, deps=(), group=None, **kwargs):
        if group is None:
            group = self.dma_group()
        needed = self._waits_for("pool", deps)
        group.val += 16
        gsem = group.sem

        def thunk(engine, needed=needed, args=args, kwargs=kwargs, gsem=gsem):
            for (s, v) in needed:
                engine.wait_ge(s, v)
            engine.indirect_dma_start(
                *args, bounds_check=bc_state["reg"], **kwargs
            ).then_inc(gsem, 16)

        self.thunks["pool"].append(thunk)
        return group

    def collective(self, kind, op, groups, in_ap, out_ap, deps=()):
        self._group_n += 1
        sem = self.new_sem(f"cc_{self._group_n}")
        needed = self._waits_for("pool", deps)

        def thunk(engine, needed=needed):
            for (s, v) in needed:
                engine.wait_ge(s, v)
            engine.collective_compute(
                kind, op, replica_groups=groups,
                ins=[in_ap.opt()], outs=[out_ap.opt()],
            ).then_inc(sem)

        self.thunks["pool"].append(thunk)
        return Tok(sem, 1)

    def bar(self, eng):
        return Tok(self.sems[eng], self.counts[eng])

    def wait_raw(self, eng, tok):
        if isinstance(tok, DmaGroup):
            tok = tok.snap()

        def thunk(engine, tok=tok):
            engine.wait_ge(tok.sem, tok.val)
        self.thunks[eng].append(thunk)

    def build(self, block):
        def run(eng_name):
            def body(engine):
                for t in self.thunks[eng_name]:
                    t(engine)
            return body
        if self.thunks["sp"]:
            block.sync(run("sp"))
        if self.thunks["pool"]:
            block.gpsimd(run("pool"))
        if self.thunks["dve"]:
            block.vector(run("dve"))
        if self.thunks["pe"]:
            block.tensor(run("pe"))
        if self.thunks["act"]:
            block.scalar(run("act"))


class Cfg:
    def __init__(self, S=8192, TPC=2048, CAP=256, INTER=768):
        self.D = 1024
        self.E = 64
        self.EPC = 8
        self.S = S
        self.TPC = TPC
        self.CAP = CAP
        self.INTER = INTER
        self.MT = self.D // 128
        self.NT = TPC // 128
        self.KT = S // 128
        self.QCH = min(512, TPC)
        self.NQC = TPC // self.QCH
        self.GCH = min(512, TPC)
        self.NGC = S // self.GCH
        self.IT = INTER // 128
        self.SLOTS = self.E * CAP
        self.SPC = self.EPC * CAP
        self.LNEPS = 1e-5
        assert TPC % 128 == 0 and S % self.GCH == 0 and self.SPC % 128 == 0
        assert self.SPC == TPC, "layout reuse assumes SPC == TPC"


def build_program(c):
    nc = bass.Bass("TRN2", target_bir_lowering=False)
    D, E, EPC, CAP, INTER = c.D, c.E, c.EPC, c.CAP, c.INTER
    MT, NT, KT, IT = c.MT, c.NT, c.KT, c.IT
    TPC, S, QCH, NQC, GCH, NGC = c.TPC, c.S, c.QCH, c.NQC, c.GCH, c.NGC

    x_own = nc.dram_tensor("x_own", [TPC, D], F32, kind="ExternalInput").ap()
    w1 = nc.dram_tensor("w1", [D, 4 * D], F32, kind="ExternalInput").ap()
    b1 = nc.dram_tensor("b1", [4 * D], F32, kind="ExternalInput").ap()
    w2 = nc.dram_tensor("w2", [D, D], F32, kind="ExternalInput").ap()
    b2 = nc.dram_tensor("b2", [D], F32, kind="ExternalInput").ap()
    gamma = nc.dram_tensor("gamma", [D], F32, kind="ExternalInput").ap()
    beta = nc.dram_tensor("beta", [D], F32, kind="ExternalInput").ap()
    gate_w = nc.dram_tensor("gate_w", [E, D], F32, kind="ExternalInput").ap()
    wg8 = nc.dram_tensor("wg8", [EPC, INTER, D], F32, kind="ExternalInput").ap()
    wu8 = nc.dram_tensor("wu8", [EPC, INTER, D], F32, kind="ExternalInput").ap()
    wd8 = nc.dram_tensor("wd8", [EPC, D, INTER], F32, kind="ExternalInput").ap()
    rankmask = nc.dram_tensor("rankmask", [1, 8], F32, kind="ExternalInput").ap()
    out = nc.dram_tensor("out", [TPC, D], F32, kind="ExternalOutput").ap()

    xnT_mine_d = nc.dram_tensor("xnT_mine_d", [D, TPC], BF16, kind="Internal").ap()
    xnT_all_d = nc.dram_tensor("xnT_all_d", [4 * D, TPC], BF16, kind="Internal").ap()
    qT_d = nc.dram_tensor("qT_d", [D, TPC], BF16, kind="Internal").ap()
    uT_d = nc.dram_tensor("uT_d", [D, TPC], BF16, kind="Internal").ap()
    T_d = nc.dram_tensor("T_d", [8, E], F32, kind="Internal").ap()
    C_d = nc.dram_tensor("C_d", [1, E], F32, kind="Internal").ap()
    R_d = nc.dram_tensor("R_d", [NT, E], F32, kind="Internal").ap()
    disp_part = nc.dram_tensor("disp_part", [c.SLOTS, D], BF16, kind="Internal").ap()
    disp_own = nc.dram_tensor("disp_own", [c.SPC, D], BF16, kind="Internal").ap()
    wgbf_d = nc.dram_tensor("wgbf_d", [EPC, INTER, D], BF16, kind="Internal").ap()
    wubf_d = nc.dram_tensor("wubf_d", [EPC, INTER, D], BF16, kind="Internal").ap()
    wdbf_d = nc.dram_tensor("wdbf_d", [EPC, D, INTER], BF16, kind="Internal").ap()
    expc_flag_d = nc.dram_tensor("expc_flag_d", [EPC, 1], BF16,
                                 kind="Internal").ap()
    y_own_d = nc.dram_tensor("y_own_d", [c.SPC, D], BF16, kind="Internal").ap()
    avT_d = nc.dram_tensor("avT_d", [D, TPC], BF16, kind="Internal").ap()
    y_all_d = nc.dram_tensor("y_all_d", [c.SLOTS, D], BF16, kind="Internal").ap()

    with ExitStack() as ctx:
        ent = ctx.enter_context

        def sbuf(name, shape, dt=BF16):
            return ent(nc.sbuf_tensor(name, shape, dt))

        ARENA = 2 * IT * 128 * 8 + IT * D
        big0 = sbuf("big0", [128, NT * D])
        big1 = sbuf("big1", [128, MT * TPC])
        arena_a = sbuf("arena_a", [128, max(2 * IT * 128 * 8, 2 * S)])
        arena_b = sbuf("arena_b", [128, max(16384, KT * 256)])
        small4 = sbuf("small4", [128, 5120])
        wslot = sbuf("wslot", [128, MT * D])
        qtg = sbuf("qtg", [128, 2 * TPC])
        ujt = sbuf("ujt", [128, 2 * MT * 128])
        y_g0 = sbuf("y_g0", [128, D])
        y_g1 = sbuf("y_g1", [128, D])
        zt = sbuf("zt", [128, 2 * 128])
        z2t = sbuf("z2t", [128, 2 * 128])
        logits = sbuf("logits", [128, max(NT * E, D)], F32)
        A_sb = sbuf("A_sb", [128, NT * E])
        idxf = sbuf("idxf", [128, NT * 8], F32)
        w8 = sbuf("w8", [128, NT * 8], F32)
        posal = sbuf("posal", [128, NT * 8], F32)
        validm = sbuf("validm", [128, NT * 8], F32)
        idx32 = sbuf("idx32", [128, NT * 8], I32)
        scr64 = sbuf("scr64", [128, E], F32)
        eq_bf = sbuf("eq_bf", [128, E])
        t8 = sbuf("t8", [128, 8], F32)
        idxu = sbuf("idxu", [128, 8], U32)
        sml = sbuf("sml", [128, 32], F32)
        statc = sbuf("statc", [128, 16], F32)
        xg = sbuf("xg", [128, D], F32)
        iota_p = sbuf("iota_p", [128, 128], F32)
        iota_f = sbuf("iota_f", [128, 128], F32)
        iota64r = sbuf("iota64r", [128, E], F32)
        identb = sbuf("identb", [128, 128])
        identf = sbuf("identf", [64, 64], F32)
        triub = sbuf("triub", [128, 128])
        triu16 = sbuf("triu16", [16, 16], F32)
        ones_row = sbuf("ones_row", [1, 128], F32)
        ones_cb = sbuf("ones_cb", [128, 1])
        zrow = sbuf("zrow", [1, 512])
        avst = sbuf("avst", [128, 4 * 512])
        row_stage = sbuf("row_stage", [1, D], F32)
        rmask = sbuf("rmask", [1, 8], F32)
        bcols = sbuf("bcols", [128, 48], F32)
        gate_b = sbuf("gate_b", [E, D])
        gwT = sbuf("gwT", [128, MT * E])
        slT = sbuf("slT", [64, NT], F32)
        cs_sb = sbuf("cs_sb", [1, E], F32)
        T_sb = sbuf("T_sb", [8, E], F32)
        sl_sb = sbuf("sl_sb", [NT, E], F32)
        R_sb = sbuf("R_sb", [NT, E], F32)
        C_sb = sbuf("C_sb", [1, E], F32)
        c_rep = sbuf("c_rep", [128, D])
        b2p_rep = sbuf("b2p_rep", [128, D])
        Rrow = logits[0:1, 0:NT * E]
        crow = logits[0:1, 0:D]
        b2prow = xg[0:1, 0:D]

        psP2 = ent(nc.psum_tensor("psP2", [128, 1024], F32))
        psQ2 = ent(nc.psum_tensor("psQ2", [128, 1024], F32))
        psR2 = ent(nc.psum_tensor("psR2", [128, 1024], F32))
        psA = ent(nc.psum_tensor("psA", [128, 512], F32))
        psB = ent(nc.psum_tensor("psB", [128, 512], F32))
        psTrA = psA[:, 0:64].bitcast(BF16)
        psTrB = psB[:, 0:64].bitcast(BF16)

        p = Prog(nc, ctx)
        block = ent(nc.Block())
        grp_all = [list(range(8))]
        grp_batch = [[0, 1, 2, 3], [4, 5, 6, 7]]

        ti1 = p.emit("pool", "iota", out=iota_p[:], pattern=[[0, 128]], base=0,
                     channel_multiplier=1, allow_small_or_imprecise_dtypes=True)
        ti2 = p.emit("pool", "iota", out=iota_f[:], pattern=[[1, 128]], base=0,
                     channel_multiplier=0, allow_small_or_imprecise_dtypes=True)
        ti3 = p.emit("pool", "iota", out=iota64r[:], pattern=[[1, E]], base=0,
                     channel_multiplier=0, allow_small_or_imprecise_dtypes=True)
        tid = p.emit("dve", "tensor_tensor", out=identb[:], in0=iota_p[:],
                     in1=iota_f[:], op=ALU.is_equal, deps=[ti1, ti2])
        tidf = p.emit("dve", "tensor_tensor", out=identf[:],
                      in0=iota_p[0:64, 0:64], in1=iota_f[0:64, 0:64],
                      op=ALU.is_equal, deps=[ti1, ti2])
        ttri = p.emit("dve", "tensor_tensor", out=triub[:], in0=iota_p[:],
                      in1=iota_f[:], op=ALU.is_lt, deps=[ti1, ti2])
        ttri16 = p.emit("dve", "tensor_tensor", out=triu16[:],
                        in0=iota_p[0:16, 0:16], in1=iota_f[0:16, 0:16],
                        op=ALU.is_lt, deps=[ti1, ti2])
        tones = p.emit("dve", "memset", ones_row[:], 1.0)
        tonesc = p.emit("dve", "memset", ones_cb[:], 1.0)
        tyg0 = p.emit("dve", "memset", y_g0[:], 0.0)
        tyg1 = p.emit("dve", "memset", y_g1[:], 0.0)
        tzrow = p.emit("dve", "memset", zrow[:], 0.0)
        zrow_bf = zrow
        ones_row_bf = sbuf("ones_row_bf", [1, 128])
        tones_bf = p.emit("dve", "memset", ones_row_bf[:], 1.0)
        teps = p.emit("dve", "memset", sml[:, 15:16], float(c.LNEPS))

        g_rows = p.dma_group("g_rows")
        p.dma("sp", out=rmask[:], in_=rankmask, group=g_rows)
        g_gate = p.dma("pool", out=gate_b[:], in_=gate_w)

        g_expc = p.dma_group("g_expc")
        g_expc_e = [p.dma_group(f"g_expc{e}") for e in range(EPC)]
        tz0 = p.emit("dve", "memset", arena_b[:, 0:16384], 0.0)
        g_zero = p.dma_group("g_zero")

        row_srcs = [(b1, 3 * D, 8), (b1, 1 * D, 8), (b1, 2 * D, 8),
                    (b1, 0, 8), (gamma, 0, 8), (beta, 0, 8)]
        g_rst = p.dma_group("g_rst")
        tcolmm = None
        coli = 0
        prev_mm = None
        for (src, off, njj) in row_srcs:
            grl = p.dma("sp", out=row_stage[:],
                        in_=src.rearrange("(o n) -> o n", o=1)[:, off:off + D],
                        deps=[prev_mm], group=g_rst)
            for jj in range(njj):
                tcolmm = p.emit("pe", "matmul", psA[:, coli:coli + 1],
                                row_stage[0:1, 128 * jj:128 * (jj + 1)],
                                ones_row[0:1, 0:1],
                                start=(coli == 0), stop=(coli == 47),
                                deps=[g_rst, tones])
                coli += 1
            prev_mm = tcolmm
        tbcols = p.emit("dve", "tensor_copy", out=bcols[:], in_=psA[:, 0:48],
                        deps=[tcolmm])
        BC_K, BC_V, BC_Q, BC_U, BC_G, BC_B = 0, 8, 16, 24, 32, 40

        tgb = g_gate
        tgwT = []
        prev_ev = [tbcols, tbcols]
        for mt in range(MT):
            ps = psTrA if mt % 2 == 0 else psTrB
            tt = p.emit("pe", "transpose", ps[:, 0:E],
                        gate_b[0:E, 128 * mt:128 * (mt + 1)],
                        identb[0:E, 0:E], deps=[tgb, tid, prev_ev[mt % 2]])
            tev = p.emit("dve", "tensor_copy", out=gwT[:, mt * E:(mt + 1) * E],
                         in_=ps[:, 0:E], deps=[tt])
            tgwT.append(tev)
            prev_ev[mt % 2] = tev

        xn_toks = [None] * NT
        xn_t_toks = {}
        xgbufs = [xg[:], logits[:, 0:D]]
        prev_xg_dve = [None, None]
        prev_xn = [None, None]
        g_x1p = [p.dma_group("g_x1a"), p.dma_group("g_x1b")]
        for jt in range(NT):
            xgb = xgbufs[jt % 2]
            so = 16 * (jt % 2)
            gx = p.dma("sp", out=xgb, in_=x_own[128 * jt:128 * (jt + 1), :],
                       deps=[prev_xn[jt % 2], prev_xg_dve[jt % 2]],
                       group=g_x1p[jt % 2])
            ts1 = p.emit("dve", "tensor_reduce", sml[:, so + 0:so + 1], xgb,
                         AX.X, ALU.add,
                         deps=[gx, xn_toks[jt - 2] if jt >= 2 else None])
            ts2 = p.emit("act", "activation", out=small4[:, (jt % 2) * 512:
                                                         (jt % 2) * 512 + 512],
                         in_=xgb[:, 0:512], func=AF.Square,
                         accum_out=sml[:, so + 1:so + 2],
                         deps=[gx, xn_toks[jt - 2] if jt >= 2 else None])
            ts2b = p.emit("act", "activation", out=small4[:, (jt % 2) * 512:
                                                          (jt % 2) * 512 + 512],
                          in_=xgb[:, 512:1024], func=AF.Square,
                          accum_out=sml[:, so + 2:so + 3], deps=[ts2])
            tm = p.emit("dve", "tensor_scalar_mul", sml[:, so + 3:so + 4],
                        sml[:, so + 0:so + 1], 1.0 / D, deps=[ts1])
            tss = p.emit("dve", "tensor_add", sml[:, so + 1:so + 2],
                         sml[:, so + 1:so + 2], sml[:, so + 2:so + 3],
                         deps=[ts2b])
            tmsq = p.emit("dve", "tensor_mul", sml[:, so + 4:so + 5],
                          sml[:, so + 3:so + 4], sml[:, so + 3:so + 4],
                          deps=[tm])
            tvar = p.emit("dve", "tensor_scalar_mul", sml[:, so + 5:so + 6],
                          sml[:, so + 1:so + 2], 1.0 / D, deps=[tss])
            tvar2 = p.emit("dve", "tensor_sub", sml[:, so + 5:so + 6],
                           sml[:, so + 5:so + 6], sml[:, so + 4:so + 5],
                           deps=[tvar, tmsq])
            tln = p.emit("act", "activation", out=sml[:, so + 6:so + 7],
                         in_=sml[:, so + 5:so + 6],
                         func=AF.Ln, bias=sml[:, 15:16], scale=1.0,
                         deps=[tvar2, teps])
            tlnh = p.emit("dve", "tensor_scalar_mul", sml[:, so + 6:so + 7],
                          sml[:, so + 6:so + 7], -0.5, deps=[tln])
            trstd = p.emit("act", "activation", out=sml[:, so + 7:so + 8],
                           in_=sml[:, so + 6:so + 7],
                           func=AF.Exp, bias=0.0, scale=1.0, deps=[tlnh])
            txn = p.emit("dve", "tensor_scalar", big0[:, jt * D:(jt + 1) * D],
                         xgb, sml[:, so + 3:so + 4], sml[:, so + 7:so + 8],
                         op0=ALU.subtract, op1=ALU.mult, deps=[trstd, tm])
            prev_xg_dve[jt % 2] = txn
            prev_xn[jt % 2] = ts2b
            xn_toks[jt] = txn
            for mt in range(MT):
                ps = psTrA if mt % 2 == 0 else psTrB
                ttr = p.emit("pe", "transpose", ps[:, 0:128],
                             big0[:, jt * D + 128 * mt: jt * D + 128 * (mt + 1)],
                             identb[:], deps=[txn, prev_ev[mt % 2]])
                tev = p.emit("dve", "tensor_copy",
                             out=big1[:, mt * TPC + jt * 128:
                                      mt * TPC + (jt + 1) * 128],
                             in_=ps[:, 0:128], deps=[ttr])
                xn_t_toks[(jt, mt)] = tev
                prev_ev[mt % 2] = tev
        xnT_done = p.bar("dve")

        for e in range(EPC):
            p.dma("pool", out=wgbf_d[e], in_=wg8[e],
                  deps=[g_x1p[0], g_x1p[1]], group=g_expc_e[e])
            p.dma("pool", out=wubf_d[e], in_=wu8[e],
                  deps=[g_x1p[0], g_x1p[1]], group=g_expc_e[e])
            p.dma("pool", out=wdbf_d[e], in_=wd8[e],
                  deps=[g_x1p[0], g_x1p[1]], group=g_expc_e[e])
            p.dma("pool", out=expc_flag_d[e:e + 1, 0:1],
                  in_=wg8[e][0:1, 0:1],
                  deps=[g_expc_e[e]], group=g_expc)
        zrows = c.SLOTS // 128
        for i in range(0, zrows, 16):
            nb = min(16, zrows - i)
            p.dma("act",
                  out=disp_part.rearrange("(r p) d -> p r d", p=128)[:, i:i + nb, :],
                  in_=arena_b[:, 0:nb * D].rearrange("p (r d) -> p r d", d=D),
                  deps=[tz0, g_x1p[0], g_x1p[1]], group=g_zero)

        g_xnst = p.dma_group("g_xnst")
        for mt in range(MT):
            p.dma("sp", out=xnT_mine_d[128 * mt:128 * (mt + 1), :],
                  in_=big1[:, mt * TPC:(mt + 1) * TPC],
                  deps=[xnT_done], group=g_xnst)
        t_ag_xnt = p.collective("AllGather", ALU.bypass, grp_batch,
                                xnT_mine_d, xnT_all_d, deps=[g_xnst])

        bc_state = {}

        def setup_bcreg(engine):
            reg = nc.alloc_register(mybir.EngineType.Pool, "bcreg")
            engine.reg_mov(reg, c.SLOTS - 1)
            bc_state["reg"] = reg
        p.thunks["pool"].append(setup_bcreg)
        g_scat = p.dma_group("g_scat")
        lg_toks = [None] * NT
        for jt in range(NT):
            ps = psA if jt % 2 == 0 else psB
            tmm = None
            for mt in range(MT):
                tmm = p.emit("pe", "matmul", ps[:, 0:E],
                             big1[:, mt * TPC + jt * 128:
                                  mt * TPC + (jt + 1) * 128],
                             gwT[:, mt * E:(mt + 1) * E],
                             start=(mt == 0), stop=(mt == MT - 1),
                             deps=[xn_t_toks[(jt, mt)], tgwT[mt], tbcols,
                                   prev_ev[0] if jt < 2 else None,
                                   prev_ev[1] if jt < 2 else None,
                                   lg_toks[jt - 2] if jt >= 2 else None])
            lg_toks[jt] = p.emit("dve", "tensor_copy",
                                 out=logits[:, jt * E:(jt + 1) * E],
                                 in_=ps[:, 0:E], deps=[tmm])
        A_toks = [None] * NT
        w8_toks = [None] * NT
        idx_toks = [None] * NT
        pos_toks = [None] * NT
        moe_steps = []
        rst = {"tsc": None, "tcs": None, "prevP": None, "tidx": None,
               "t_rs_disp": None}

        def step_top8(jt):
            sl = slice(jt * E, (jt + 1) * E)
            tmax = p.emit("dve", "max", out=t8[:], in_=logits[:, sl],
                          deps=[lg_toks[jt]])
            tmi = p.emit("dve", "max_index", out=idxu[:], in_max=t8[:],
                         in_values=logits[:, sl], deps=[tmax])
            tixf = p.emit("dve", "tensor_copy",
                          out=idxf[:, jt * 8:(jt + 1) * 8],
                          in_=idxu[:], deps=[tmi])
            idx_toks[jt] = tixf
            tneg = p.emit("dve", "tensor_scalar_mul", sml[:, 9:10], t8[:, 0:1],
                          -1.0, deps=[tmax])
            texp = p.emit("act", "activation", out=t8[:], in_=t8[:],
                          func=AF.Exp, bias=sml[:, 9:10], scale=1.0,
                          deps=[tneg, tmi])
            tsw = p.emit("dve", "tensor_reduce", sml[:, 10:11], t8[:], AX.X,
                         ALU.add, deps=[texp])
            twi = p.emit("dve", "reciprocal", sml[:, 11:12], sml[:, 10:11],
                         deps=[tsw])
            w8_toks[jt] = p.emit("dve", "tensor_single_scalar",
                                 w8[:, jt * 8:(jt + 1) * 8], t8[:],
                                 sml[:, 11:12], ALU.mult, deps=[twi])
            ta = None
            for k in range(8):
                te = p.emit("dve", "tensor_single_scalar", eq_bf[:],
                            iota64r[:], idxf[:, jt * 8 + k: jt * 8 + k + 1],
                            ALU.is_equal, deps=[tixf, ti3, ta])
                if k == 0:
                    ta = p.emit("dve", "tensor_copy",
                                out=A_sb[:, sl], in_=eq_bf[:], deps=[te])
                else:
                    ta = p.emit("dve", "tensor_add", A_sb[:, sl], A_sb[:, sl],
                                eq_bf[:], deps=[te])
            A_toks[jt] = ta

        def step_colsumS(jt):
            rst["tsc"] = p.emit("pe", "matmul", psB[0:E, jt:jt + 1],
                                A_sb[:, jt * E:(jt + 1) * E], ones_cb[:],
                                start=(jt == 0), stop=(jt == NT - 1),
                                deps=[A_toks[jt], tonesc])

        def step_slT_evac():
            rst["tslT"] = p.emit("dve", "tensor_copy", out=slT[:, 0:NT],
                                 in_=psB[0:E, 0:NT], deps=[rst["tsc"]])

        def step_colsumC(jt):
            rst["tcs"] = p.emit("pe", "matmul", psB[0:1, 0:E], ones_cb[:],
                                A_sb[:, jt * E:(jt + 1) * E],
                                start=(jt == 0), stop=(jt == NT - 1),
                                deps=[A_toks[jt], rst["tslT"]])

        def step_TRC():
            tcsr = p.emit("dve", "tensor_copy", out=cs_sb[:],
                          in_=psB[0:1, 0:E], deps=[rst["tcs"]])
            tT = p.emit("pe", "matmul", psB[0:8, 0:E], rmask[:], cs_sb[:],
                        start=True, stop=True, deps=[tcsr, g_rows])
            tTe = p.emit("dve", "tensor_copy", out=T_sb[:], in_=psB[0:8, 0:E],
                         deps=[tT])
            g_T = p.dma("sp", out=T_d, in_=T_sb[:], deps=[tTe])
            rst["t_rs_c"] = p.collective("ReduceScatter", ALU.add, grp_all,
                                         T_d, C_d, deps=[g_T])
            rst["tTe"] = tTe
            rst["tcsr"] = tcsr

        def step_R():
            g_C = p.dma("sp", out=C_sb[:], in_=C_d, deps=[rst["t_rs_c"]])
            tslt = p.emit("pe", "transpose", psB[0:NT, 0:E], slT[:, 0:NT],
                          identf[0:E, 0:E],
                          deps=[rst["tslT"], rst["tTe"], tidf])
            tsle = p.emit("dve", "tensor_copy", out=sl_sb[:],
                          in_=psB[0:NT, 0:E], deps=[tslt])
            tR1 = p.emit("pe", "matmul", psB[0:NT, 0:E], triu16[0:NT, 0:NT],
                         sl_sb[:], start=True, stop=False,
                         deps=[tsle, ttri16, rst["tcsr"]])
            tR2 = p.emit("pe", "matmul", psB[0:NT, 0:E], ones_row[0:1, 0:NT],
                         C_sb[:], start=False, stop=True, deps=[g_C])
            tRe = p.emit("dve", "tensor_copy", out=R_sb[:], in_=psB[0:NT, 0:E],
                         deps=[tR2])
            g_R = p.dma("sp", out=R_d, in_=R_sb[:], deps=[tRe])
            rst["g_Rr"] = p.dma("sp", out=Rrow[:],
                                in_=R_d.rearrange("(o a) b -> o (a b)", o=1),
                                deps=[g_R])
            rst["prevP"] = tRe

        def step_P(jt):
            tp1 = p.emit("pe", "matmul", psB[:, 0:E], triub[:],
                         A_sb[:, jt * E:(jt + 1) * E], start=True, stop=False,
                         deps=[A_toks[jt], ttri, rst["prevP"]])
            tp2 = p.emit("pe", "matmul", psB[:, 0:E], ones_row[:],
                         Rrow[0:1, jt * E:(jt + 1) * E], start=False,
                         stop=True, deps=[rst["g_Rr"]])
            tlast = None
            for k in range(8):
                te = p.emit("dve", "tensor_single_scalar", eq_bf[:],
                            iota64r[:], idxf[:, jt * 8 + k: jt * 8 + k + 1],
                            ALU.is_equal, deps=[idx_toks[jt], tlast])
                tmu = p.emit("dve", "tensor_tensor", out=scr64[:],
                             in0=psB[:, 0:E], in1=eq_bf[:], op=ALU.mult,
                             deps=[te, tp2])
                tlast = p.emit("dve", "tensor_reduce",
                               posal[:, jt * 8 + k: jt * 8 + k + 1],
                               scr64[:], AX.X, ALU.add, deps=[tmu])
            pos_toks[jt] = tlast
            rst["prevP"] = tlast

        def step_idx():
            NK = NT * 8
            tval = p.emit("dve", "tensor_scalar", validm[:, 0:NK],
                          posal[:, 0:NK], float(CAP), None, op0=ALU.is_lt,
                          deps=pos_toks)
            rst["twm"] = p.emit("dve", "tensor_mul", w8[:, 0:NK], w8[:, 0:NK],
                                validm[:, 0:NK], deps=[tval] + w8_toks)
            tidx1 = p.emit("dve", "tensor_scalar", idxf[:, 0:NK],
                           idxf[:, 0:NK], float(CAP), None, op0=ALU.mult,
                           deps=idx_toks)
            tidx2 = p.emit("dve", "tensor_add", idxf[:, 0:NK], idxf[:, 0:NK],
                           posal[:, 0:NK], deps=[tidx1] + pos_toks)
            tidx3 = p.emit("dve", "tensor_scalar", idxf[:, 0:NK],
                           idxf[:, 0:NK], -9999999.0, None, op0=ALU.add,
                           deps=[tidx2])
            tidx4 = p.emit("dve", "tensor_mul", idxf[:, 0:NK], idxf[:, 0:NK],
                           validm[:, 0:NK], deps=[tidx3, tval])
            tidx5 = p.emit("dve", "tensor_scalar", idxf[:, 0:NK],
                           idxf[:, 0:NK], 9999999.0, None, op0=ALU.add,
                           deps=[tidx4])
            rst["tidx"] = p.emit("dve", "tensor_copy", out=idx32[:, 0:NK],
                                 in_=idxf[:, 0:NK], deps=[tidx5])

        def step_scat(jt):
            for k in range(8):
                p.indirect_dma_breg(
                    bc_state,
                    out=disp_part,
                    out_offset=IndirectOffsetOnAxis(
                        idx32[:, jt * 8 + k: jt * 8 + k + 1], 0),
                    in_=big0[:, jt * D:(jt + 1) * D], in_offset=None,
                    oob_is_err=False,
                    deps=[rst["tidx"], xn_toks[jt], g_zero, g_expc],
                    group=g_scat)

        def step_rs_disp():
            rst["t_rs_disp"] = p.collective(
                "ReduceScatter", ALU.add, grp_all, disp_part, disp_own,
                deps=[g_scat])

        for jt in range(NT):
            moe_steps.append(lambda jt=jt: step_top8(jt))
        for jt in range(NT):
            moe_steps.append(lambda jt=jt: step_colsumS(jt))
        moe_steps.append(step_slT_evac)
        for jt in range(NT):
            moe_steps.append(lambda jt=jt: step_colsumC(jt))
        moe_steps.append(step_TRC)
        moe_steps.append(step_R)
        for jt in range(NT):
            moe_steps.append(lambda jt=jt: step_P(jt))
        moe_steps.append(step_idx)
        for jt in range(NT):
            moe_steps.append(lambda jt=jt: step_scat(jt))
        moe_steps.append(step_rs_disp)
        mstate = {"i": 0}

        def drain_moe(n):
            while n > 0 and mstate["i"] < len(moe_steps):
                moe_steps[mstate["i"]]()
                mstate["i"] += 1
                n -= 1

        n_pts = 2 * NQC * MT
        ph1 = 3 * NT + 1
        p1pts = max(1, (n_pts * 27) // 100)
        gappts = max(1, (n_pts * 12) // 100)
        plan = []
        cum = 0
        c1 = (ph1 + p1pts - 1) // p1pts
        for i in range(p1pts):
            take = min(c1, ph1 - cum)
            plan.append(take)
            cum += take
        for i in range(gappts):
            plan.append(0)
        n_steps_proj = len(moe_steps) - NT - 1
        rem = n_steps_proj - cum
        rem_pts = max(1, n_pts - len(plan))
        c2 = (rem + rem_pts - 1) // rem_pts
        for i in range(rem_pts):
            plan.append(c2)
        drain_pt = {"i": 0}

        def proj_hook():
            i = drain_pt["i"]
            drain_pt["i"] += 1
            n = plan[i] if i < len(plan) else 0
            drain_moe(min(n, n_steps_proj - mstate["i"]))

        def proj_to_dram(name, bc_base, dest_d, wcols_base, prev_stores=(None, None)):
            gw = p.dma_group(f"g_w{name}")
            p.dma("pool", out=wslot[:].rearrange("p (mt k) -> p mt k", k=D),
                  in_=w1[:, wcols_base:wcols_base + D]
                  .rearrange("(mt p) k -> p mt k", p=128),
                  deps=[p.bar("pe")], group=gw)
            gst_par = [p.dma_group(f"g_st{name}0"), p.dma_group(f"g_st{name}1")]
            ab = p.bar("act")
            silu_toks = [ab] * 4
            store_snaps = list(prev_stores)
            for ch in range(NQC):
                for ct in range(MT):
                    slot = ct % 4
                    ps = psP2 if slot < 2 else psQ2
                    pview = ps[:, (slot % 2) * 512: (slot % 2) * 512 + QCH]
                    tmm = None
                    for mt in range(MT):
                        tmm = p.emit(
                            "pe", "matmul", pview,
                            wslot[:, mt * D + ct * 128: mt * D + (ct + 1) * 128],
                            big1[:, mt * TPC + ch * QCH:
                                 mt * TPC + (ch + 1) * QCH],
                            start=(mt == 0), stop=(mt == MT - 1),
                            deps=[gw, xnT_done, silu_toks[slot]])
                    buf = (ct % 2) * QCH
                    tsil = p.emit("act", "activation",
                                  out=small4[:, buf:buf + QCH],
                                  in_=pview, func=AF.Silu,
                                  bias=bcols[:, bc_base + ct:bc_base + ct + 1],
                                  scale=1.0,
                                  deps=[tmm, tbcols, store_snaps[ct % 2]])
                    silu_toks[slot] = tsil
                    p.dma("sp", out=dest_d[128 * ct:128 * (ct + 1),
                                           ch * QCH:(ch + 1) * QCH],
                          in_=small4[:, buf:buf + QCH], deps=[tsil],
                          group=gst_par[ct % 2])
                    store_snaps[ct % 2] = gst_par[ct % 2].snap()
                    proj_hook()
            return gst_par

        g_q2 = proj_to_dram("q", BC_Q, qT_d, 2 * D)
        g_u2 = proj_to_dram("u", BC_U, uT_d, 0,
                            prev_stores=(g_q2[0].snap(), g_q2[1].snap()))
        g_q = g_q2[0]
        g_qb = g_q2[1]
        g_u = g_u2[0]
        g_ub = g_u2[1]
        drain_moe(n_steps_proj - mstate["i"])

        WG_O, WU_O, WD_O = 0, IT * 128 * 8, 0
        NDC = D // 512 if D >= 512 else 1
        DCH = min(512, D)
        ICH = [(i * 512, min(INTER, (i + 1) * 512))
               for i in range((INTER + 511) // 512)]
        g_dispT = p.dma_group("g_dispT")
        g_ew_par = [p.dma_group("g_ewA"), p.dma_group("g_ewB")]
        g_ew = [g_ew_par[e % 2] for e in range(EPC)]
        g_yst = p.dma_group("g_yst")
        exp_pe_done = [None] * EPC
        comb = {"i": 0, "last_par": [None, None], "last_tacc": None,
                "jt_done": [None] * NT,
                "grp": [p.dma_group("g_comb0"), p.dma_group("g_comb1")]}
        exs = {}

        def emit_dispT(pe_tok):
            big1_free = [pe_tok, xnT_done, g_xnst]
            for ht in range(MT):
                p.dma("act", out=big1[:, ht * TPC:(ht + 1) * TPC],
                      in_=disp_own[:, 128 * ht:128 * (ht + 1)], transpose=True,
                      deps=[rst["t_rs_disp"]] + big1_free, group=g_dispT)

        def emit_expert_phase():
            bar0_pe = p.bar("pe")
            bar0_act = p.bar("act")
            bar0_dve = p.bar("dve")
            prev_tsg = bar0_act
            prev_tgu = bar0_act
            y_ev_toks = []
            y_store_snap = None
            exp_gu_done = [None] * EPC
            g_wds = [p.dma_group("g_wds0"), p.dma_group("g_wds1")]
            wd_slot_tok = [bar0_pe, bar0_pe]
            tr_prev = [bar0_dve, bar0_dve]
            tr_flip = [0]
            for e in range(EPC):
                ar = arena_a
                deps_w = [g_expc, bar0_pe]
                if e >= 1:
                    deps_w.append(exp_gu_done[e - 1])
                wd_off = (e % 2) * IT * D
                for ht in range(MT):
                    p.dma("act",
                          out=ar[:, WG_O + ht * INTER: WG_O + (ht + 1) * INTER],
                          in_=wgbf_d[e][:, 128 * ht:128 * (ht + 1)],
                          transpose=True, deps=deps_w, group=g_ew[e])
                    p.dma("act",
                          out=ar[:, WU_O + ht * INTER: WU_O + (ht + 1) * INTER],
                          in_=wubf_d[e][:, 128 * ht:128 * (ht + 1)],
                          transpose=True, deps=deps_w, group=g_ew[e])
                wd_done = None
                for i in range(MT):
                    s = (e * MT + i) % 2
                    stg = small4[:, 3328 + s * INTER: 3328 + (s + 1) * INTER]
                    p.dma("sp", out=stg,
                          in_=wdbf_d[e][128 * i:128 * (i + 1), :],
                          deps=[g_expc_e[e], wd_slot_tok[s]], group=g_wds[s])
                    ttr = None
                    for j in range(IT):
                        pst = psTrA if tr_flip[0] % 2 == 0 else psTrB
                        tr_flip[0] += 1
                        ttr = p.emit(
                            "pe", "transpose", pst[:, 0:128],
                            small4[:, 3328 + s * INTER + j * 128:
                                   3328 + s * INTER + (j + 1) * 128],
                            identb[:],
                            deps=[g_wds[s], tr_prev[0 if pst is psTrA else 1]])
                        wd_done = p.emit(
                            "dve", "tensor_copy",
                            out=arena_b[:, wd_off + j * D + 128 * i:
                                        wd_off + j * D + 128 * (i + 1)],
                            in_=pst[:, 0:128], deps=[ttr, g_zero])
                        tr_prev[0 if pst is psTrA else 1] = wd_done
                    wd_slot_tok[s] = ttr
                for st in range(CAP // 128 if CAP >= 128 else 1):
                    sbase = e * CAP + st * 128
                    srows = min(128, CAP)
                    tg = tu = None
                    for ht in range(MT):
                        lhs = big1[:, ht * TPC + sbase:
                                   ht * TPC + sbase + srows]
                        for (tgt, wo, is_g) in ((psP2, WG_O, True),
                                                (psQ2, WU_O, False)):
                            for (n0, n1) in ICH:
                                tt = p.emit(
                                    "pe", "matmul", tgt[0:srows, n0:n1], lhs,
                                    ar[:, wo + ht * INTER + n0:
                                       wo + ht * INTER + n1],
                                    start=(ht == 0), stop=(ht == MT - 1),
                                    deps=[g_dispT, g_ew[e],
                                          prev_tsg if is_g else prev_tgu])
                                if is_g:
                                    tg = tt
                                else:
                                    tu = tt
                    tsg = p.emit("act", "activation",
                                 out=small4[0:srows, 0:INTER],
                                 in_=psP2[0:srows, 0:INTER], func=AF.Silu,
                                 deps=[tg, prev_tgu])
                    prev_tsg = tsg
                    tgu = p.emit("dve", "tensor_tensor",
                                 out=small4[0:srows, INTER:2 * INTER],
                                 in0=small4[0:srows, 0:INTER],
                                 in1=psQ2[0:srows, 0:INTER], op=ALU.mult,
                                 deps=[tsg, tu])
                    prev_tgu = tgu
                    gut_toks = []
                    for it in range(IT):
                        pst = psTrA if tr_flip[0] % 2 == 0 else psTrB
                        tr_flip[0] += 1
                        ttr = p.emit("pe", "transpose", pst[:, 0:srows],
                                     small4[0:srows, INTER + it * 128:
                                            INTER + (it + 1) * 128],
                                     identb[0:srows, 0:srows],
                                     deps=[tgu,
                                           tr_prev[0 if pst is psTrA else 1]])
                        tev = p.emit("dve", "tensor_copy",
                                     out=small4[:, 2 * INTER + it * 128:
                                                2 * INTER + it * 128 + srows],
                                     in_=pst[:, 0:srows], deps=[ttr])
                        gut_toks.append(tev)
                        tr_prev[0 if pst is psTrA else 1] = tev
                    ty = None
                    for it in range(IT):
                        for nd in range(NDC):
                            ty = p.emit(
                                "pe", "matmul",
                                psR2[0:srows, nd * DCH:(nd + 1) * DCH],
                                small4[:, 2 * INTER + it * 128:
                                       2 * INTER + it * 128 + srows],
                                arena_b[:, wd_off + it * D + nd * DCH:
                                        wd_off + it * D + (nd + 1) * DCH],
                                start=(it == 0), stop=(it == IT - 1),
                                deps=[gut_toks[it], wd_done,
                                      y_ev_toks[-1] if y_ev_toks
                                      else bar0_act])
                    YO = 2 * INTER + IT * 128
                    tyev = p.emit("dve", "tensor_copy",
                                  out=small4[0:srows, YO:YO + D],
                                  in_=psR2[0:srows, 0:D],
                                  deps=[ty, y_store_snap])
                    y_ev_toks.append(tyev)
                    p.dma("sp", out=y_own_d[sbase:sbase + srows, :],
                          in_=small4[0:srows, YO:YO + D], deps=[tyev],
                          group=g_yst)
                    y_store_snap = g_yst.snap()
                    exp_gu_done[e] = tu
                exp_pe_done[e] = p.bar("pe")
            exs["t_ag_y"] = p.collective("AllGather", ALU.bypass, grp_all,
                                         y_own_d, y_all_d, deps=[g_yst])
            tmz = p.emit("dve", "memset", big0[:], 0.0,
                         deps=[g_scat, p.bar("pool")])
            exs["tmz"] = tmz
            comb["last_par"] = [tmz, tmz]
            comb["last_tacc"] = tmz

        def emit_combine(n):
            for _ in range(n):
                i = comb["i"]
                if i >= NT * 8:
                    return
                comb["i"] += 1
                jt, k = i // 8, i % 8
                par = i % 2
                yg = y_g0 if par == 0 else y_g1
                g = p.indirect_dma_breg(
                    bc_state,
                    out=yg[:], out_offset=None,
                    in_=y_all_d,
                    in_offset=IndirectOffsetOnAxis(
                        idx32[:, jt * 8 + k: jt * 8 + k + 1], 0),
                    oob_is_err=False,
                    deps=[exs["t_ag_y"], rst["tidx"], comb["last_par"][par],
                          tyg0, tyg1],
                    group=comb["grp"][par])
                tacc = p.emit("dve", "scalar_tensor_tensor",
                              out=big0[:, jt * D:(jt + 1) * D],
                              in0=yg[:],
                              scalar=w8[:, jt * 8 + k: jt * 8 + k + 1],
                              in1=big0[:, jt * D:(jt + 1) * D],
                              op0=ALU.mult, op1=ALU.add,
                              deps=[g, rst["twm"], exs["tmz"],
                                    comb["last_tacc"]])
                comb["last_par"][par] = tacc
                comb["last_tacc"] = tacc
                if k == 7:
                    comb["jt_done"][jt] = tacc

        N_AT = 2 * 2 * NQC
        comb_per = (NT * 8 + N_AT - 1) // N_AT
        n_steps = len(moe_steps)
        avT_ev = []
        g_avst = [p.dma_group(f"g_avst{i}") for i in range(4)]
        ablk = {"i": 0}
        g_qtg_grp = p.dma_group("g_qtg")
        glx_par = [p.dma_group(f"g_glx{i}") for i in range(8)]
        for g in range(4):
            if g == 1:
                drain_moe(n_steps)
                emit_expert_phase()
            bar_pe = p.bar("pe")
            bar_act = p.bar("act")
            bar_dve = p.bar("dve")
            gw = p.dma_group(f"g_kv{g}")
            p.dma("pool",
                  out=wslot[:].rearrange("p (mt k) -> p mt k", k=D)[:, :, 0:256],
                  in_=w1[:, 3 * D + 256 * g: 3 * D + 256 * (g + 1)]
                  .rearrange("(mt p) k -> p mt k", p=128),
                  deps=[bar_pe], group=gw)
            p.dma("pool",
                  out=wslot[:].rearrange("p (mt k) -> p mt k", k=D)[:, :, 256:512],
                  in_=w1[:, 1 * D + 256 * g: 1 * D + 256 * (g + 1)]
                  .rearrange("(mt p) k -> p mt k", p=128),
                  deps=[bar_pe], group=gw)

            PAR_N = min(8, (2 * TPC) // GCH)
            NCB = S // GCH
            arena_deps = [bar_act, bar_dve, g_zero, g_u, g_ub,
                          g_q, g_qb] \
                if g == 0 else \
                ([exp_pe_done[EPC - 1], bar_act, bar_dve, g_yst]
                 if g == 1 else [bar_act, bar_dve])
            pe_mt = [bar_pe] * PAR_N
            kv_bank_k = [gw, gw]
            kv_bank_v = [[gw, gw], [gw, gw]]
            sm_free = [[bar_pe, bar_pe], [bar_pe, bar_pe]]
            prev_tv = [bar_dve, bar_dve]
            pend_v = []
            CHQ = ["sp", "pool", "act"]

            def flush_vev():
                while pend_v:
                    tsil, cbq, ctq = pend_v.pop(0)
                    sbase = ((cbq % 2) * 2 + ctq) * GCH
                    ttr = None
                    for sub in range(GCH // 128):
                        kt = (cbq * GCH + sub * 128) // 128
                        pst = psTrA if sub % 2 == 0 else psTrB
                        ttr = p.emit(
                            "pe", "transpose", pst[:, 0:128],
                            small4[:, sbase + sub * 128:
                                   sbase + (sub + 1) * 128],
                            identb[:], deps=[tsil, prev_tv[sub % 2]])
                        tve = p.emit(
                            "dve", "tensor_copy",
                            out=arena_b[:, kt * 256 + ctq * 128:
                                        kt * 256 + (ctq + 1) * 128],
                            in_=pst[:, 0:128], deps=[ttr])
                        prev_tv[sub % 2] = tve
                    sm_free[cbq % 2][ctq] = ttr

            for cb in range(NCB):
                rb = (cb * GCH) // TPC
                tloc = cb * GCH - rb * TPC
                tlast = None
                for mt in range(MT):
                    par = (cb * MT + mt) % PAR_N
                    p.dma(CHQ[par % 3],
                          out=qtg[:, par * GCH:(par + 1) * GCH],
                          in_=xnT_all_d[rb * D + 128 * mt:
                                        rb * D + 128 * (mt + 1),
                                        tloc:tloc + GCH],
                          deps=[t_ag_xnt, pe_mt[par], bar_act, g_qtg_grp],
                          group=glx_par[par])
                    for pas in range(4):
                        is_v = pas >= 2
                        ct = pas % 2
                        col_off = (256 if is_v else 0) + ct * 128
                        psv = psQ2 if cb % 2 == 0 else psR2
                        dst = (psP2 if not is_v else psv)[
                            :, ct * 512: ct * 512 + GCH]
                        bank_dep = (kv_bank_k[ct] if not is_v
                                    else kv_bank_v[cb % 2][ct])
                        tlast = p.emit(
                            "pe", "matmul", dst,
                            wslot[:, mt * D + col_off:
                                  mt * D + col_off + 128],
                            qtg[:, par * GCH:(par + 1) * GCH],
                            start=(mt == 0), stop=(mt == MT - 1),
                            deps=[gw, glx_par[par], bank_dep] + arena_deps)
                    pe_mt[par] = tlast
                flush_vev()
                for ct in range(2):
                    t = p.emit("act", "activation",
                               out=arena_a[:, ct * S + cb * GCH:
                                           ct * S + (cb + 1) * GCH],
                               in_=psP2[:, ct * 512: ct * 512 + GCH],
                               func=AF.Silu,
                               bias=bcols[:, BC_K + 2 * g + ct:
                                          BC_K + 2 * g + ct + 1],
                               scale=1.0, deps=[tlast, tbcols])
                    kv_bank_k[ct] = t
                for ct in range(2):
                    psv = psQ2 if cb % 2 == 0 else psR2
                    t = p.emit("act", "activation",
                               out=small4[:, ((cb % 2) * 2 + ct) * GCH:
                                          ((cb % 2) * 2 + ct + 1) * GCH],
                               in_=psv[:, ct * 512: ct * 512 + GCH],
                               func=AF.Silu,
                               bias=bcols[:, BC_V + 2 * g + ct:
                                          BC_V + 2 * g + ct + 1],
                               scale=1.0,
                               deps=[tlast, tbcols, sm_free[cb % 2][ct]])
                    kv_bank_v[cb % 2][ct] = t
                    pend_v.append((t, cb, ct))
            flush_vev()
            kT_ready = p.bar("act")
            v_ready = p.bar("dve")
            kv_pe = p.bar("pe")
            silu_t = [None, None, None]
            av_t = [None, None, None]
            g_qt = p.dma("sp", out=qtg[:].rearrange("p (hp t) -> p hp t", hp=2),
                         in_=qT_d[256 * g:256 * (g + 1), :]
                         .rearrange("(hp pp) t -> pp hp t", pp=128),
                         deps=[g_q, g_qb, p.bar("pe")], group=g_qtg_grp)
            big1_pe_tok = p.bar("pe")
            scps = [psP2, psQ2, psR2]
            for hp in range(2):
                for qc in range(NQC):
                    q0 = qc * QCH

                    def sc_mm(kt):
                        scp = scps[kt % 3]
                        p.emit("pe", "matmul", scp[:, 0:QCH],
                               arena_a[0:64, hp * S + kt * 128:
                                       hp * S + (kt + 1) * 128],
                               qtg[0:64, hp * TPC + q0: hp * TPC + q0 + QCH],
                               start=True, stop=True, tile_position=(0, 0),
                               deps=[kT_ready, g_qt, silu_t[kt % 3], kv_pe,
                                     v_ready],
                               track=False)
                        return p.emit("pe", "matmul", scp[:, 512:512 + QCH],
                                      arena_a[64:128, hp * S + kt * 128:
                                              hp * S + (kt + 1) * 128],
                                      qtg[64:128, hp * TPC + q0:
                                          hp * TPC + q0 + QCH],
                                      start=True, stop=True,
                                      tile_position=(64, 0))

                    def silu_mm(kt, t2):
                        scp = scps[kt % 3]
                        if QCH == 512:
                            return p.emit("act", "activation",
                                          out=small4[:, (kt % 3) * 1024:
                                                     (kt % 3) * 1024 + 1024],
                                          in_=scp[:, 0:1024], func=AF.Silu,
                                          deps=[t2, av_t[kt % 3]])
                        p.emit("act", "activation",
                               out=small4[:, (kt % 3) * 1024:
                                          (kt % 3) * 1024 + QCH],
                               in_=scp[:, 0:QCH], func=AF.Silu,
                               deps=[t2, av_t[kt % 3]])
                        return p.emit("act", "activation",
                                      out=small4[:, (kt % 3) * 1024 + 512:
                                                 (kt % 3) * 1024 + 512 + QCH],
                                      in_=scp[:, 512:512 + QCH],
                                      func=AF.Silu)

                    def av_open():
                        return p.emit("pe", "matmul", psA[:, 0:QCH],
                                      ones_row_bf[0:1, 0:128],
                                      zrow_bf[0:1, 0:QCH],
                                      start=True, stop=False,
                                      skip_group_check=True,
                                      deps=[tzrow, tones_bf, v_ready,
                                            avT_ev[-1] if avT_ev else None])

                    def av_mm(kt, tsil):
                        par = kt % 3
                        ta1 = p.emit("pe", "matmul", psA[0:64, 0:QCH],
                                     arena_b[:, kt * 256 + hp * 128:
                                             kt * 256 + hp * 128 + 64],
                                     small4[:, par * 1024: par * 1024 + QCH],
                                     start=False, stop=(kt == KT - 1),
                                     tile_position=(0, 0),
                                     skip_group_check=True,
                                     deps=[tsil, v_ready])
                        ta2 = p.emit("pe", "matmul", psA[64:128, 0:QCH],
                                     arena_b[:, kt * 256 + hp * 128 + 64:
                                             kt * 256 + hp * 128 + 128],
                                     small4[:, par * 1024 + 512:
                                            par * 1024 + 512 + QCH],
                                     start=False, stop=(kt == KT - 1),
                                     tile_position=(0, 64),
                                     skip_group_check=True)
                        av_t[par] = ta2
                        return ta1, ta2

                    sils = {}
                    av_open()
                    for kt in range(min(2, KT)):
                        t2 = sc_mm(kt)
                        sils[kt] = silu_mm(kt, t2)
                        silu_t[kt % 3] = sils[kt]
                    for kt in range(2, KT):
                        t2 = sc_mm(kt)
                        av_mm(kt - 2, sils.pop(kt - 2))
                        sils[kt] = silu_mm(kt, t2)
                        silu_t[kt % 3] = sils[kt]
                    for kt in range(max(0, KT - 2), KT):
                        tav1, tav = av_mm(kt, sils.pop(kt))
                    apar = ablk["i"] % 4
                    ablk["i"] += 1
                    tev = p.emit("dve", "tensor_copy",
                                 out=avst[:, apar * 512: apar * 512 + QCH],
                                 in_=psA[:, 0:QCH],
                                 deps=[tav1, tav, g_avst[apar]])
                    p.dma("sp", out=avT_d[(2 * g + hp) * 128:
                                          (2 * g + hp) * 128 + 128,
                                          q0:q0 + QCH],
                          in_=avst[:, apar * 512: apar * 512 + QCH],
                          deps=[tev], group=g_avst[apar])
                    avT_ev.append(tev)
                    if g == 0:
                        drain_moe(3)
                    elif g >= 2:
                        emit_combine(comb_per)
            if g == 0:
                drain_moe(n_steps)
                emit_dispT(big1_pe_tok)
        emit_combine(NT * 8)
        att_pe = p.bar("pe")
        att_act = p.bar("act")

        gw2 = p.dma_group("g_w2")
        w2_fold = [None] * MT
        tb2p = None
        tcmm = None
        prev_xg_pe = None
        for mt in range(MT):
            gl = p.dma("sp", out=xg[:], in_=w2[128 * mt:128 * (mt + 1), :],
                       deps=[att_pe, att_act, w2_fold[mt - 1] if mt else None,
                             prev_xg_pe,
                             comb["last_tacc"] if mt == 0 else None],
                       group=gw2)
            for nd in range(NDC):
                tb2p = p.emit("pe", "matmul",
                              psP2[0:1, nd * DCH:(nd + 1) * DCH],
                              bcols[:, BC_B + mt: BC_B + mt + 1],
                              xg[:, nd * DCH:(nd + 1) * DCH],
                              start=(mt == 0), stop=(mt == MT - 1),
                              deps=[gw2, tbcols, att_pe])
            tfold = p.emit("dve", "tensor_single_scalar",
                           wslot[:, mt * D:(mt + 1) * D], xg[:],
                           bcols[:, BC_G + mt: BC_G + mt + 1], ALU.mult,
                           deps=[gw2, tbcols])
            w2_fold[mt] = tfold
            prev_xg_pe = tb2p
            for nd in range(NDC):
                tcmm = p.emit("pe", "matmul",
                              psQ2[0:1, nd * DCH:(nd + 1) * DCH],
                              ones_cb[:],
                              wslot[:, mt * D + nd * DCH:
                                    mt * D + (nd + 1) * DCH],
                              start=(mt == 0), stop=(mt == MT - 1),
                              deps=[tfold, tonesc])
        g_b2l = p.dma("sp", out=row_stage[:],
                      in_=b2.rearrange("(o n) -> o n", o=1),
                      deps=[prev_mm, g_rst])
        tb2pr = p.emit("dve", "tensor_add", b2prow[:], psP2[0:1, 0:D],
                       row_stage[:], deps=[tb2p, g_b2l, w2_fold[MT - 1]])
        tcrow = p.emit("dve", "tensor_copy", out=crow[:], in_=psQ2[0:1, 0:D],
                       deps=[tcmm])
        tbc1 = None
        for nd in range(NDC):
            tbc1 = p.emit("pe", "matmul", psP2[:, nd * DCH:(nd + 1) * DCH],
                          ones_row[:], b2prow[0:1, nd * DCH:(nd + 1) * DCH],
                          start=True, stop=True,
                          deps=[tb2pr, tcrow])
        tb2rep = p.emit("dve", "tensor_copy", out=b2p_rep[:], in_=psP2[:, 0:D],
                        deps=[tbc1])
        tbc2 = None
        for nd in range(NDC):
            tbc2 = p.emit("pe", "matmul", psQ2[:, nd * DCH:(nd + 1) * DCH],
                          ones_row[:], crow[0:1, nd * DCH:(nd + 1) * DCH],
                          start=True, stop=True,
                          deps=[tcrow, tb2rep])
        tcrep = p.emit("dve", "tensor_copy", out=c_rep[:], in_=psQ2[:, 0:D],
                       deps=[tbc2])

        INV2 = 1.0 / float(S)
        g_out = p.dma_group("g_out")
        prev_rd = [{"st": tcrep, "y": tb2rep}, {"st": tcrep, "y": tcrep}]
        prev_ujt = [tcrep, tcrep]
        gul_par = [p.dma_group("g_ul0"), p.dma_group("g_ul1")]
        g_xre_par = [p.dma_group("g_xrea"), p.dma_group("g_xreb")]
        prev_epi = [tcrep, tcrep]
        xg_z = [xg[:], logits[:, 0:D]]
        ypss = [psP2, psQ2]
        for jt in range(NT):
            pp = jt % 2
            so = 8 * pp
            yps = ypss[pp]
            stA = psA[:, 0:1] if pp == 0 else psR2[:, 0:1]
            stB = psB[:, 0:1] if pp == 0 else psR2[:, 512:513]
            g_ul = gul_par[pp]
            p.dma("sp", out=ujt[:, pp * MT * 128:(pp + 1) * MT * 128]
                  .rearrange("p (mt t) -> p mt t", t=128),
                  in_=uT_d.rearrange("(mt p) t -> p mt t", p=128)[
                      :, :, jt * 128:(jt + 1) * 128],
                  deps=[g_u, g_ub, prev_ujt[pp]], group=g_ul)
            AVJ = 3072 + pp * 1024
            p.dma("sp", out=small4[:, AVJ:AVJ + 1024]
                  .rearrange("p (mt t) -> p mt t", t=128),
                  in_=avT_d.rearrange("(mt p) t -> p mt t", p=128)[
                      :, :, jt * 128:(jt + 1) * 128],
                  deps=[g_avst[0], g_avst[1], g_avst[2], g_avst[3],
                        prev_ujt[pp]], group=g_ul)
            g_xre = p.dma("sp", out=xg_z[pp],
                          in_=x_own[128 * jt:128 * (jt + 1), :],
                          deps=[prev_epi[pp], g_out], group=g_xre_par[pp])
            tzs = tss2 = tw2 = None
            zpar_tok = [None, None]
            last_z_dve = None
            for mt in range(MT):
                zp = mt % 2
                tz = p.emit("dve", "tensor_tensor",
                            out=zt[:, zp * 128:(zp + 1) * 128],
                            in0=small4[:, 3072 + pp * 1024 + mt * 128:
                                       3072 + pp * 1024 + (mt + 1) * 128],
                            in1=ujt[:, pp * MT * 128 + mt * 128:
                                    pp * MT * 128 + (mt + 1) * 128],
                            op=ALU.mult,
                            deps=[g_ul, avT_ev[-1], zpar_tok[zp]])
                tz2 = p.emit("dve", "tensor_tensor",
                             out=z2t[:, zp * 128:(zp + 1) * 128],
                             in0=zt[:, zp * 128:(zp + 1) * 128],
                             in1=zt[:, zp * 128:(zp + 1) * 128],
                             op=ALU.mult, deps=[tz])
                last_z_dve = tz2
                tzs = p.emit("pe", "matmul", stA,
                             zt[:, zp * 128:(zp + 1) * 128], ones_cb[:],
                             start=(mt == 0), stop=(mt == MT - 1),
                             deps=[tz, tonesc,
                                   prev_rd[pp]["st"] if mt == 0 else None])
                tss2 = p.emit("pe", "matmul", stB,
                              zt[:, zp * 128:(zp + 1) * 128] if False else
                              z2t[:, zp * 128:(zp + 1) * 128], ones_cb[:],
                              start=(mt == 0), stop=(mt == MT - 1),
                              deps=[tz2,
                                    prev_rd[pp]["st"] if mt == 0 else None])
                for nd in range(NDC):
                    tw2 = p.emit("pe", "matmul",
                                 yps[:, nd * DCH:(nd + 1) * DCH],
                                 zt[:, zp * 128:(zp + 1) * 128],
                                 wslot[:, mt * D + nd * DCH:
                                       mt * D + (nd + 1) * DCH],
                                 start=(mt == 0), stop=(mt == MT - 1),
                                 deps=[tz, w2_fold[mt],
                                       prev_rd[pp]["y"] if mt == 0 and nd == 0
                                       else None])
                zpar_tok[zp] = tw2
            t1 = p.emit("dve", "tensor_copy", out=statc[:, so + 0:so + 1],
                        in_=stA, deps=[tzs])
            t2 = p.emit("dve", "tensor_copy", out=statc[:, so + 1:so + 2],
                        in_=stB, deps=[tss2])
            t3 = p.emit("dve", "tensor_scalar_mul", statc[:, so + 2:so + 3],
                        statc[:, so + 0:so + 1], 1.0 / D, deps=[t1])
            t4 = p.emit("dve", "tensor_mul", statc[:, so + 3:so + 4],
                        statc[:, so + 2:so + 3], statc[:, so + 2:so + 3],
                        deps=[t3])
            t5 = p.emit("dve", "tensor_scalar_mul", statc[:, so + 4:so + 5],
                        statc[:, so + 1:so + 2], 1.0 / D, deps=[t2])
            t6 = p.emit("dve", "tensor_sub", statc[:, so + 4:so + 5],
                        statc[:, so + 4:so + 5], statc[:, so + 3:so + 4],
                        deps=[t5, t4])
            t6b = p.emit("dve", "tensor_scalar_mul", statc[:, so + 4:so + 5],
                         statc[:, so + 4:so + 5], float(INV2 * INV2),
                         deps=[t6])
            t7 = p.emit("act", "activation", out=statc[:, so + 5:so + 6],
                        in_=statc[:, so + 4:so + 5], func=AF.Ln,
                        bias=sml[:, 15:16], scale=1.0, deps=[t6b, teps])
            t7b = p.emit("dve", "tensor_scalar_mul", statc[:, so + 5:so + 6],
                         statc[:, so + 5:so + 6], -0.5, deps=[t7])
            t8b = p.emit("act", "activation", out=statc[:, so + 6:so + 7],
                         in_=statc[:, so + 5:so + 6], func=AF.Exp, bias=0.0,
                         scale=1.0, deps=[t7b])
            t9 = p.emit("dve", "tensor_scalar_mul", statc[:, so + 6:so + 7],
                        statc[:, so + 6:so + 7], float(INV2), deps=[t8b])
            t10 = p.emit("dve", "tensor_mul", statc[:, so + 7:so + 8],
                         statc[:, so + 2:so + 3], statc[:, so + 6:so + 7],
                         deps=[t9, t3])
            t11 = p.emit("dve", "tensor_scalar_mul", statc[:, so + 7:so + 8],
                         statc[:, so + 7:so + 8], -1.0, deps=[t10])
            last = None
            for nd in range(NDC):
                slx = slice(nd * DCH, (nd + 1) * DCH)
                o1 = p.emit("dve", "scalar_tensor_tensor", out=xg_z[pp][:, slx],
                            in0=yps[:, slx], scalar=statc[:, so + 6:so + 7],
                            in1=xg_z[pp][:, slx], op0=ALU.mult, op1=ALU.add,
                            deps=[tw2, t9, g_xre, last])
                o2 = p.emit("dve", "scalar_tensor_tensor", out=xg_z[pp][:, slx],
                            in0=c_rep[:, slx], scalar=statc[:, so + 7:so + 8],
                            in1=xg_z[pp][:, slx], op0=ALU.mult, op1=ALU.add,
                            deps=[o1, t11, tcrep])
                o3 = p.emit("dve", "tensor_add", xg_z[pp][:, slx],
                            xg_z[pp][:, slx], b2p_rep[:, slx],
                            deps=[o2, tb2rep])
                last = p.emit("dve", "tensor_add", xg_z[pp][:, slx],
                              xg_z[pp][:, slx],
                              big0[:, jt * D + nd * DCH:
                                   jt * D + (nd + 1) * DCH],
                              deps=[o3, comb["jt_done"][jt], exs["tmz"]])
            p.dma("sp", out=out[128 * jt:128 * (jt + 1), :], in_=xg_z[pp][:],
                  deps=[last], group=g_out)
            prev_rd[pp] = {"st": t2, "y": last}
            prev_ujt[pp] = last_z_dve
            prev_epi[pp] = last
        p.wait_raw("sp", g_out)
        p.build(block)
    return nc


_CFG = Cfg()
_NC_CACHE = {}


def _get_nc(cfg):
    key = (cfg.S, cfg.TPC, cfg.CAP, cfg.INTER)
    if key not in _NC_CACHE:
        _NC_CACHE[key] = build_program(cfg)
    return _NC_CACHE[key]


def make_in_maps(cfg, L0_attn_x, L0_attn_W1, L0_attn_b1, L0_attn_W2,
                 L0_attn_b2, L0_attn_gamma, L0_attn_beta, L0_ffn_gate_w,
                 L0_ffn_w_gate, L0_ffn_w_up, L0_ffn_w_down):
    c = cfg
    xflat = np.ascontiguousarray(
        np.asarray(L0_attn_x, dtype=np.float32).reshape(-1, c.D))
    w1 = np.ascontiguousarray(np.asarray(L0_attn_W1, dtype=np.float32))
    b1 = np.ascontiguousarray(np.asarray(L0_attn_b1, dtype=np.float32))
    w2 = np.ascontiguousarray(np.asarray(L0_attn_W2, dtype=np.float32))
    b2 = np.ascontiguousarray(np.asarray(L0_attn_b2, dtype=np.float32))
    gm = np.ascontiguousarray(np.asarray(L0_attn_gamma, dtype=np.float32))
    bt = np.ascontiguousarray(np.asarray(L0_attn_beta, dtype=np.float32))
    gw = np.ascontiguousarray(np.asarray(L0_ffn_gate_w, dtype=np.float32))
    wgt = np.asarray(L0_ffn_w_gate, dtype=np.float32)
    wup = np.asarray(L0_ffn_w_up, dtype=np.float32)
    wdn = np.asarray(L0_ffn_w_down, dtype=np.float32)
    in_maps = []
    for r in range(8):
        rm = (np.arange(8)[None, :] > r).astype(np.float32)
        in_maps.append({
            "x_own": np.ascontiguousarray(xflat[r * c.TPC:(r + 1) * c.TPC]),
            "w1": w1, "b1": b1, "w2": w2, "b2": b2,
            "gamma": gm, "beta": bt, "gate_w": gw,
            "wg8": np.ascontiguousarray(wgt[r * c.EPC:(r + 1) * c.EPC]),
            "wu8": np.ascontiguousarray(wup[r * c.EPC:(r + 1) * c.EPC]),
            "wd8": np.ascontiguousarray(wdn[r * c.EPC:(r + 1) * c.EPC]),
            "rankmask": np.ascontiguousarray(rm),
        })
    return in_maps


def kernel(**inputs):
    c = _CFG
    nc = _get_nc(c)
    in_maps = make_in_maps(c, **inputs)
    res = run_bass_kernel_spmd(nc, in_maps, core_ids=list(range(8)))
    shards = [np.asarray(res.results[r]["out"]) for r in range(8)]
    full = np.concatenate(shards, axis=0).reshape(2, c.S, c.D)
    return full.astype(np.float32)
